# version 07
# baseline (speedup 1.0000x reference)
import contextlib
import numpy as np
import concourse.bass as bass
import concourse.mybir as mybir
from concourse.bass_utils import run_bass_kernel_spmd

F32 = mybir.dt.float32
BF16 = mybir.dt.bfloat16
AF = mybir.ActivationFunctionType
ALU = mybir.AluOpType
AX = mybir.AxisListType

S, D, H, DH, FF = 4096, 1024, 16, 64, 4096
T = 512
NT = S // T
NS = T // 128
DC = D // 128
FC = FF // 128
EPS_LN = 1e-5
EPS_Z = 1e-6
KB = 1024

ARENA = 16512
OFF = dict(xbuf=ARENA, res1=ARENA + 16 * KB, res2=ARENA + 32 * KB,
           fmT=ARENA + 48 * KB, scrF=ARENA + 56 * KB, bigA=ARENA + 72 * KB,
           bigB=ARENA + 104 * KB, wbuf=ARENA + 136 * KB, consts=ARENA + 168 * KB)


def _roundup(x, a):
    return (x + a - 1) // a * a


class Emitter:
    def __init__(self):
        self.ev = {k: [] for k in ("sync", "gpsimd", "scalar", "vector", "tensor")}
        self.ctr = {}
        self.sems = {}
        self.bounce_cache = {}

    def sem_val(self, sem):
        return self.ctr.get(sem.name, 0)

    def bounce(self, sem, val):
        key = (sem.name, val)
        if key in self.bounce_cache:
            return self.bounce_cache[key]
        via = "scalar" if sem.name == "sV" else "vector"
        via_sem = self.sems["sS"] if via == "scalar" else self.sems["sV"]
        self.ev[via].append((None, [(sem, val)], None))
        self.ctr[via_sem.name] = self.ctr.get(via_sem.name, 0) + 1
        self.ev[via].append((lambda eng_: eng_.drain(), [], (via_sem, 1)))
        out = (via_sem, self.ctr[via_sem.name])
        self.bounce_cache[key] = out
        return out

    def emit(self, eng, fn, waits=(), inc=None, amt=1, flush=False):
        w = []
        for (sem, val) in waits:
            if val is None or val <= 0:
                continue
            assert val <= self.ctr.get(sem.name, 0), \
                f"wait {sem.name}>={val} but only {self.ctr.get(sem.name, 0)} emitted"
            w.append((sem, val))
        if inc is not None and amt == 1 and eng in ("vector", "scalar", "gpsimd"):
            self.ev[eng].append((fn, w, None))
            self.ctr[inc.name] = self.ctr.get(inc.name, 0) + amt
            self.ev[eng].append((lambda eng_: eng_.drain(), [], (inc, amt)))
            return self.ctr.get(inc.name, 0)
        incv = None
        if inc is not None:
            self.ctr[inc.name] = self.ctr.get(inc.name, 0) + amt
            incv = (inc, amt)
        self.ev[eng].append((fn, w, incv))
        return self.ctr.get(inc.name, 0) if inc is not None else None

    def barrier(self, eng, waits):
        self.emit(eng, None, waits=waits)


def build(stage="full", n_tiles=NT):
    nc = bass.Bass("TRN2", target_bir_lowering=False, debug=False)

    x_h = nc.dram_tensor("x", [S, D], F32, kind="ExternalInput").ap()
    out_h = nc.dram_tensor("out", [S, D], F32, kind="ExternalOutput").ap()

    wf = {}
    for name, shape in [
        ("wk_t", [128, DC, D]), ("wv_t", [128, DC, D]), ("wo_t", [128, DC, D]),
        ("wq_s", [DC, 128, D]),
        ("w1_s", [FC, 128, D]), ("w2_s", [DC, 128, FF]),
        ("fw1_s", [FC, 128, D]), ("fw2_s", [FC, 128, D]),
        ("h1_s", [FC, 128, FF]), ("h2_s", [FC, 128, FF]),
        ("w3_s", [DC, 128, FF]),
    ]:
        wf[name] = nc.dram_tensor(name, shape, F32, kind="ExternalInput").ap()
    wb = {name: nc.dram_tensor(name + "_b", list(ap.shape), BF16, kind="Internal").ap()
          for name, ap in wf.items()}

    vecs = {name: nc.dram_tensor(name, [D], F32, kind="ExternalInput").ap()
            for name in ("bk", "bv", "bo", "g1", "be1", "g2", "be2")}
    cols_h = {}
    for name, w in [("bq_c", DC), ("b2_c", DC), ("s3_c", DC), ("b1_c", FC),
                    ("s1_c", FC), ("s2_c", FC), ("sh1_c", FC), ("sh2_c", FC)]:
        cols_h[name] = nc.dram_tensor(name, [128, w], F32, kind="ExternalInput").ap()
    id_h = nc.dram_tensor("id_f32", [128, 128], F32, kind="ExternalInput").ap()
    sel_h = nc.dram_tensor("sel_f32", [16, 1024], F32, kind="ExternalInput").ap()

    A = {}

    def sba(name, shape, dtype, off):
        A[name] = nc.alloc_sbuf_tensor_at(name, shape, dtype, offset=off).ap()

    sba("xbuf", [128, 4096], F32, OFF["xbuf"])
    sba("res1", [128, 4096], F32, OFF["res1"])
    sba("res2", [128, 4096], F32, OFF["res2"])
    sba("fmT", [128, 4096], BF16, OFF["fmT"])
    sba("kbf", [128, 4096], BF16, OFF["scrF"])
    sba("vbf", [128, 4096], BF16, OFF["scrF"] + 8 * KB)
    sba("qbf", [128, 4096], BF16, OFF["scrF"])
    sba("obf", [128, 4096], BF16, OFF["scrF"] + 8 * KB)
    sba("scrF", [128, 4096], F32, OFF["scrF"])
    sba("bigA", [128, 16384], BF16, OFF["bigA"])
    sba("bigB", [128, 16384], BF16, OFF["bigB"])
    sba("selA", [16, 1024], F32, OFF["bigA"] + 28 * KB)
    sba("zbsbB", [128, 512], F32, OFF["bigB"] + 30 * KB)
    for i in range(4):
        sba(f"wbuf{i}", [128, 4096], BF16, OFF["wbuf"] + i * 8 * KB)

    co = OFF["consts"]

    def calloc(name, shape, dtype):
        nonlocal co
        n = int(np.prod(shape[1:]))
        sba(name, shape, dtype, co)
        co += _roundup(n * (2 if dtype == BF16 else 4), 32)

    for nm in ("bkB", "bvB", "boB", "g1B", "be1B", "g2B", "be2B"):
        calloc(nm, [128, 1024], F32)
    for nm, w in [("bq_c", DC), ("b2_c", DC), ("s3_c", DC), ("b1_c", FC),
                  ("s1_c", FC), ("s2_c", FC), ("sh1_c", FC), ("sh2_c", FC)]:
        calloc(nm, [128, w], F32)
    calloc("idf", [128, 128], F32)
    calloc("kvsb", [128, 512], BF16)
    calloc("ksml", [128, 128], BF16)
    calloc("zsb", [16, 512], F32)
    calloc("zbsb2", [128, 512], F32)
    calloc("ones", [128, 1], BF16)
    calloc("stats", [128, 32], F32)
    calloc("pad8", [128, 8], F32)
    calloc("scr4k", [128, 1024], F32)
    assert co <= 229344, f"consts overflow: {co}"

    P = [nc.alloc_psum_tensor(f"p{i}", [128, 512], F32).ap() for i in range(8)]

    E = Emitter()
    stack = contextlib.ExitStack()
    qW = [stack.enter_context(nc.semaphore(f"qW{i}")) for i in range(4)]
    qK = stack.enter_context(nc.semaphore("qK"))
    qSel = stack.enter_context(nc.semaphore("qSel"))
    qX = stack.enter_context(nc.semaphore("qX"))
    qC = stack.enter_context(nc.semaphore("qC"))
    sP = stack.enter_context(nc.semaphore("sP"))
    sV = stack.enter_context(nc.semaphore("sV"))
    sS = stack.enter_context(nc.semaphore("sS"))
    E.sems = {"sV": sV, "sS": sS}

    for nm, dst in [("bk", "bkB"), ("bv", "bvB"), ("bo", "boB"), ("g1", "g1B"),
                    ("be1", "be1B"), ("g2", "g2B"), ("be2", "be2B")]:
        E.emit("sync", (lambda d=A[dst], s=vecs[nm]: lambda eng: eng.dma_start(
            out=d[:], in_=s.unsqueeze(0).partition_broadcast(128)))(), inc=qK, amt=16)
    for nm in ("bq_c", "b2_c", "s3_c", "b1_c", "s1_c", "s2_c", "sh1_c", "sh2_c"):
        E.emit("sync", (lambda d=A[nm], s=cols_h[nm]: lambda eng: eng.dma_start(
            out=d[:], in_=s))(), inc=qK, amt=16)
    E.emit("sync", lambda eng: eng.dma_start(out=A["idf"][:], in_=id_h), inc=qK, amt=16)
    consts_loaded = E.sem_val(qK)

    cast_done = {}
    for name in ("wk_t", "wv_t", "wq_s", "wo_t", "w1_s", "w2_s",
                 "fw1_s", "fw2_s", "h1_s", "h2_s", "w3_s"):
        src, dst = wf[name], wb[name]
        n0 = src.shape[0]
        pieces = 4 if int(np.prod(src.shape)) * 4 > 20e6 else 1
        step = n0 // pieces
        for i in range(pieces):
            prev = E.sem_val(qC)
            E.emit("gpsimd", (lambda d=dst, s=src, a=i * step, b=(i + 1) * step:
                              lambda eng: eng.dma_start(out=d[a:b], in_=s[a:b]))(),
                   waits=[(qC, prev)], inc=qC, amt=16)
        cast_done[name] = E.sem_val(qC)

    ring = {"idx": 0, "loadedv": {}, "freed": [None] * 4}

    def load_chunk(dram_ap, matrix):
        slot = ring["idx"] % 4
        waits = [(qC, cast_done[matrix])]
        if ring["freed"][slot] is not None:
            waits.append(ring["freed"][slot])
        if ring.get("extra"):
            waits.append(ring["extra"])
        if len(dram_ap.shape) == 3:
            dview = A[f"wbuf{slot}"].rearrange("p (a d) -> p a d", d=dram_ap.shape[2])
        else:
            dview = A[f"wbuf{slot}"][:]
        E.emit("sync", (lambda d=dview, s=dram_ap:
                        lambda eng: eng.dma_start(out=d, in_=s))(),
               waits=waits, inc=qW[slot], amt=16)
        tok = ring["idx"]
        ring["loadedv"][tok] = (qW[slot], E.sem_val(qW[slot]))
        ring["idx"] += 1
        return slot, tok

    def chunk_wait(tok):
        return ring["loadedv"][tok]

    def chunk_consumed(tok, slot, sem, val):
        ring["freed"][slot] = (sem, val)

    def ln_stats(src, scratch, src_ready=None, base=0):
        st = A["stats"]
        b = base
        for s in range(NS):
            ws = list(src_ready or []) if s == 0 else []
            E.emit("scalar", (lambda s=s, scr=scratch: lambda eng: eng.activation(
                scr[:, s * 1024:(s + 1) * 1024], src[:, s * 1024:(s + 1) * 1024],
                AF.Square))(), waits=ws,
                inc=sS if s == NS - 1 else None, flush=(s == NS - 1))
        sq_done = E.sem_val(sS)
        for s in range(NS):
            E.emit("vector", (lambda s=s, b=b: lambda eng: eng.reduce_sum(
                st[:, b + s:b + s + 1], src[:, s * 1024:(s + 1) * 1024], axis=AX.X))(),
                inc=sV)
        for s in range(NS):
            E.emit("vector", (lambda s=s, b=b, scr=scratch: lambda eng: eng.reduce_sum(
                st[:, b + 4 + s:b + 5 + s], scr[:, s * 1024:(s + 1) * 1024], axis=AX.X))(),
                waits=[(sS, sq_done)] if s == 0 else (), inc=sV)
        for j in range(3):
            E.emit("vector", (lambda j=j: lambda eng: eng.memset(A["pad8"][:], float(j)))())
        E.emit("vector", (lambda b=b: lambda eng: eng.tensor_scalar(
            st[:, b:b + 4], st[:, b:b + 4], 1.0 / D, 0.0, ALU.mult, ALU.add))(), inc=sV)
        E.emit("vector", (lambda b=b: lambda eng: eng.tensor_scalar(
            st[:, b + 4:b + 8], st[:, b + 4:b + 8], 1.0 / D, EPS_LN, ALU.mult, ALU.add))(), inc=sV)
        E.emit("vector", (lambda b=b: lambda eng: eng.tensor_tensor(
            st[:, b + 8:b + 12], st[:, b:b + 4], st[:, b:b + 4], ALU.mult))(), inc=sV)
        E.emit("vector", (lambda b=b: lambda eng: eng.tensor_tensor(
            st[:, b + 4:b + 8], st[:, b + 4:b + 8], st[:, b + 8:b + 12], ALU.subtract))(), inc=sV)
        v_var = E.sem_val(sV)
        E.emit("scalar", (lambda b=b: lambda eng: eng.activation(
            st[:, b + 4:b + 8], st[:, b + 4:b + 8], AF.Sqrt))(),
            waits=[(sV, v_var)], inc=sS, flush=True)
        s_sqrt = E.sem_val(sS)
        E.emit("vector", (lambda b=b: lambda eng: eng.reciprocal(
            st[:, b + 4:b + 8], st[:, b + 4:b + 8]))(),
               waits=[(sS, s_sqrt)], inc=sV)
        return E.sem_val(sV)

    def ln_apply(src, dst, stats_ready, base=0, extra_waits=()):
        st = A["stats"]
        for s in range(NS):
            w = ([(sV, stats_ready)] + list(extra_waits)) if s == 0 else []
            E.emit("vector", (lambda s=s, b=base: lambda eng: eng.tensor_scalar(
                dst[:, s * 1024:(s + 1) * 1024], src[:, s * 1024:(s + 1) * 1024],
                st[:, b + s:b + s + 1], st[:, b + 4 + s:b + 5 + s],
                ALU.subtract, ALU.mult))(), waits=w, inc=sV)
        return E.sem_val(sV)

    tr_banks = {"p1": (4, 5), "p2": (6, 7)}
    tr_free = {4: None, 5: None, 6: None, 7: None}
    rot_free = {i: None for i in range(6)}

    def transpose_sweep(src, which, ready_w, evac):
        b0, b1 = tr_banks[which]
        last_v = 0
        for s in range(NS):
            for half in range(2):
                bank = b0 if half == 0 else b1
                waits = list(ready_w)
                if tr_free[bank] is not None:
                    waits.append(tr_free[bank])
                for j in range(4):
                    c = half * 4 + j
                    E.emit("tensor", (lambda s=s, c=c, j=j, bank=bank, src=src:
                                      lambda eng: eng.transpose(
                        P[bank][:, j * 128:(j + 1) * 128],
                        src[:, ln_tr_in(s, c)], A["idf"][:]))(),
                        waits=waits if j == 0 else (),
                        inc=sP if j == 3 else None)
                p_done = E.sem_val(sP)
                last_v = evac(s, half, bank, p_done)
                tr_free[bank] = (sV, last_v)
        return last_v

    def ln_tr_in(s, c):
        return slice(s * 1024 + c * 128, s * 1024 + (c + 1) * 128)

    def tm_to_fm(src, which, ready_w, dst):
        def evac(s, half, bank, p_done):
            E.emit("vector", (lambda s=s, half=half, bank=bank, d=dst:
                              lambda eng: eng.tensor_copy(
                d.rearrange("p (c x) -> p c x", x=512)[:, half * 4:half * 4 + 4,
                                                       s * 128:s * 128 + 128],
                P[bank].rearrange("p (j x) -> p j x", x=128)))(),
                waits=[(sP, p_done)], inc=sV)
            return E.sem_val(sV)
        return transpose_sweep(src, which, ready_w, evac)

    def fm_to_tm(src_fm, which, ready_w, dst_tm, add):
        b0, b1 = tr_banks[which]
        last_v = 0
        for s in range(NS):
            for half in range(2):
                bank = b0 if half == 0 else b1
                waits = list(ready_w)
                if tr_free[bank] is not None:
                    waits.append(tr_free[bank])
                for j in range(4):
                    c = half * 4 + j
                    E.emit("tensor", (lambda s=s, c=c, j=j, bank=bank, src=src_fm:
                                      lambda eng: eng.transpose(
                        P[bank][:, j * 128:(j + 1) * 128],
                        src[:, c * 512 + s * 128: c * 512 + s * 128 + 128],
                        A["idf"][:]))(),
                        waits=waits if j == 0 else (),
                        inc=sP if j == 3 else None)
                p_done = E.sem_val(sP)
                dview = dst_tm.rearrange("p (s c x) -> p s c x", s=NS, x=128)
                if add:
                    E.emit("vector", (lambda s=s, half=half, bank=bank, dv=dview:
                                      lambda eng: eng.tensor_tensor(
                        dv[:, s, half * 4:half * 4 + 4, :],
                        dv[:, s, half * 4:half * 4 + 4, :],
                        P[bank].rearrange("p (j x) -> p j x", x=128), ALU.add))(),
                        waits=[(sP, p_done)], inc=sV)
                else:
                    E.emit("vector", (lambda s=s, half=half, bank=bank, dv=dview:
                                      lambda eng: eng.tensor_copy(
                        dv[:, s, half * 4:half * 4 + 4, :],
                        P[bank].rearrange("p (j x) -> p j x", x=128)))(),
                        waits=[(sP, p_done)], inc=sV)
                last_v = E.sem_val(sV)
                tr_free[bank] = (sV, last_v)
        return last_v

    def mm_group(bank, matmuls, waits=()):
        n = len(matmuls)
        for i, fn in enumerate(matmuls):
            w = []
            if i == 0:
                if rot_free[bank] is not None:
                    w.append(rot_free[bank])
                w += list(waits)
            E.emit("tensor", (lambda fn=fn, i=i, bank=bank, n=n:
                              lambda eng: eng.matmul(
                P[bank][:, 0:512], fn[0](), fn[1](),
                start=(i == 0), stop=(i == n - 1)))(),
                waits=w, inc=sP if i == n - 1 else None)
        return E.sem_val(sP)

    x_tiled = x_h.rearrange("(t s p) d -> t p s d", p=128, s=NS)
    out_tiled = out_h.rearrange("(t s p) d -> t p s d", p=128, s=NS)
    xbuf_3d = A["xbuf"].rearrange("p (s d) -> p s d", d=D)

    state = {"xbuf_free": None, "store_done": None, "kv_first": True,
             "ks_first": True}

    def pass1_tile(t):
        waits = []
        if state["xbuf_free"] is not None:
            waits.append(state["xbuf_free"])
        E.emit("scalar", (lambda t=t: lambda eng: eng.dma_start(
            out=xbuf_3d[:], in_=x_tiled[t]))(), waits=waits, inc=qX, amt=16)
        x_ld = E.sem_val(qX)
        E.barrier("vector", [(qX, x_ld), (qK, consts_loaded if t == 0 else 0)])
        E.barrier("scalar", [(qX, x_ld)])
        stats_v = ln_stats(A["xbuf"], A["res2"])
        u_v = ln_apply(A["xbuf"], A["res1"], stats_v)
        state["xbuf_free"] = (sV, u_v)
        fmT_v = tm_to_fm(A["res1"], "p1", [(sV, u_v)], A["fmT"])

        def proj_tm(wname, dst_bf, is_k):
            s0, t0c = load_chunk(wb[wname][:, 0:4, :], wname)
            s1, t1c = load_chunk(wb[wname][:, 4:8, :], wname)
            lastV = 0
            bank_i = 0
            for s in range(NS):
                for g in range(2):
                    bank = bank_i % 4
                    bank_i += 1
                    mats = []
                    for c in range(DC):
                        slot = s0 if c < 4 else s1
                        cc = c % 4
                        mats.append((
                            (lambda c=c, s=s: A["fmT"][:, c * 512 + s * 128: c * 512 + s * 128 + 128]),
                            (lambda slot=slot, cc=cc, g=g: A[f"wbuf{slot}"][:, cc * 1024 + g * 512: cc * 1024 + g * 512 + 512]),
                        ))
                    g_done = mm_group(bank, mats,
                                      waits=[(sV, fmT_v), chunk_wait(t0c), chunk_wait(t1c)])
                    off = s * 1024 + g * 512
                    if is_k:
                        E.emit("vector", (lambda bank=bank, g=g: lambda eng: eng.tensor_tensor(
                            A["scr4k"][:, 0:512], P[bank][:, 0:512],
                            A["bkB"][:, g * 512:(g + 1) * 512], ALU.add))(),
                            waits=[(sP, g_done)], inc=sV)
                        v_tt = E.sem_val(sV)
                        rot_free[bank] = (sV, v_tt)
                        E.emit("vector", lambda eng: eng.tensor_scalar(
                            A["scr4k"][:, 512:1024], A["scr4k"][:, 0:512],
                            0.0, 0.0, ALU.min, ALU.add), inc=sV)
                        v_min = E.sem_val(sV)
                        E.emit("scalar", lambda eng: eng.activation(
                            A["scr4k"][:, 512:1024], A["scr4k"][:, 512:1024], AF.Exp),
                            waits=[(sV, v_min)], inc=sS, flush=True)
                        s_exp = E.sem_val(sS)
                        E.emit("vector", (lambda off=off, dst=dst_bf: lambda eng: eng.scalar_tensor_tensor(
                            dst[:, off:off + 512], A["scr4k"][:, 0:512], 0.0,
                            A["scr4k"][:, 512:1024], ALU.max, ALU.add))(),
                            waits=[(sS, s_exp)], inc=sV)
                        lastV = E.sem_val(sV)
                    else:
                        E.emit("vector", (lambda bank=bank, g=g, off=off, dst=dst_bf:
                                          lambda eng: eng.tensor_tensor(
                            dst[:, off:off + 512], P[bank][:, 0:512],
                            A["bvB"][:, g * 512:(g + 1) * 512], ALU.add))(),
                            waits=[(sP, g_done)], inc=sV)
                        lastV = E.sem_val(sV)
                        rot_free[bank] = (sV, lastV)
            chunk_consumed(t0c, s0, sV, lastV)
            chunk_consumed(t1c, s1, sV, lastV)
            return lastV

        k_done = proj_tm("wk_t", A["kbf"], True)
        v_done = proj_tm("wv_t", A["vbf"], False)

        if t == 0:
            E.emit("vector", lambda eng: eng.memset(A["ones"][:], 1.0), inc=sV)
        ones_v = E.sem_val(sV)

        kv_p = 0
        for s in range(NS):
            for h in range(H):
                pb = (h % 2) * 64
                c = h // 2
                w = [(sV, max(k_done, v_done, ones_v))] if (s == 0 and h == 0) else []
                last = (t == n_tiles - 1 and s == NS - 1 and h == H - 1)
                E.emit("tensor", (lambda s=s, h=h, pb=pb, c=c, last=last: lambda eng: eng.matmul(
                    P[6][pb:pb + 64, c * 64:(c + 1) * 64],
                    A["kbf"][:, s * 1024 + h * 64: s * 1024 + (h + 1) * 64],
                    A["vbf"][:, s * 1024 + h * 64: s * 1024 + (h + 1) * 64],
                    start=state["kv_first"], stop=last))(), waits=w)
                state["kv_first"] = False
                kv_p = E.emit("tensor", (lambda s=s, h=h, pb=pb, c=c, last=last: lambda eng: eng.matmul(
                    P[7][pb:pb + 64, c:c + 1],
                    A["kbf"][:, s * 1024 + h * 64: s * 1024 + (h + 1) * 64],
                    A["ones"][:, 0:1],
                    start=state["ks_first"], stop=last))(),
                    inc=sP if last else None)
                state["ks_first"] = False
        return kv_p

    for t in range(n_tiles):
        kv_p = pass1_tile(t)

    E.barrier("scalar", [(sP, kv_p)])
    E.emit("scalar", lambda eng: eng.drain(), inc=sS)
    kv_sS = E.sem_val(sS)
    E.emit("vector", lambda eng: eng.tensor_copy(A["kvsb"][:], P[6][:]),
           waits=[(sP, kv_p), (sS, kv_sS)], inc=sV)
    E.emit("vector", lambda eng: eng.memset(A["ksml"][:], 0.0), inc=sV)
    for h in range(H):
        pb = (h % 2) * 64
        c = h // 2
        E.emit("vector", (lambda h=h, pb=pb, c=c: lambda eng: eng.tensor_copy(
            A["ksml"][pb:pb + 64, c * 16 + h: c * 16 + h + 1],
            P[7][pb:pb + 64, c:c + 1]))(), inc=sV)
    kv_ready = E.sem_val(sV)
    tr_free[6] = (sV, kv_ready)
    tr_free[7] = (sV, kv_ready)
    rot_free[4] = tr_free[4]
    rot_free[5] = tr_free[5]

    def dbg_store(t, tile, v):
        E.emit("scalar", (lambda t=t, tile=tile: lambda eng: eng.dma_start(
            out=out_tiled[t], in_=tile.rearrange("p (s d) -> p s d", d=D)))(),
               waits=[(sV, v)], inc=qX, amt=16)
        state["store_done"] = (qX, E.sem_val(qX))
        state["xbuf_free"] = (sV, v)
        state["sel_gate"] = state["store_done"]

    def pass2_tile(t):
        import os
        if os.environ.get("SERIAL_LOADS"):
            ring["extra"] = state["store_done"]
        waits = []
        if state["xbuf_free"] is not None:
            waits.append(state["xbuf_free"])
        if state["store_done"] is not None:
            waits.append(state["store_done"])
        selw = [] if state.get("sel_gate") is None else [state["sel_gate"]]
        E.emit("sync", lambda eng: eng.dma_start(out=A["selA"][:], in_=sel_h),
               waits=selw, inc=qSel, amt=16)
        sel_ld = E.sem_val(qSel)
        if state.get("early") is not None:
            stats_v, sd = state.pop("early")
            u_v = ln_apply(A["xbuf"], A["res1"], stats_v, base=16,
                           extra_waits=[sd, state["store_done"]])
        else:
            E.emit("scalar", (lambda t=t: lambda eng: eng.dma_start(
                out=xbuf_3d[:], in_=x_tiled[t]))(), waits=waits, inc=qX, amt=16)
            x_ld = E.sem_val(qX)
            E.barrier("vector", [(qX, x_ld)])
            E.barrier("scalar", [(qX, x_ld)])
            stats_v = ln_stats(A["xbuf"], A["res2"])
            u_v = ln_apply(A["xbuf"], A["res1"], stats_v)
        fmT_v = tm_to_fm(A["res1"], "p2", [(sV, u_v)], A["fmT"])
        if stage == "u":
            return dbg_store(t, A["res1"], u_v)

        flatq = wb["wq_s"].rearrange("a p d -> p a d")
        q0s, q0t = load_chunk(flatq[:, 0:4, :], "wq_s")
        q1s, q1t = load_chunk(flatq[:, 4:8, :], "wq_s")
        q_done_v = 0
        for cq in range(DC):
            bank = cq % 4
            slot, tok = (q0s, q0t) if cq < 4 else (q1s, q1t)
            ccol = (cq % 4) * 1024
            mats = [((lambda slot=slot, ccol=ccol, c=c: A[f"wbuf{slot}"][:, ccol + c * 128: ccol + (c + 1) * 128]),
                     (lambda c=c: A["fmT"][:, c * 512:(c + 1) * 512]))
                    for c in range(DC)]
            g_done = mm_group(bank, mats, waits=[(sV, fmT_v), chunk_wait(tok)])
            E.emit("vector", (lambda bank=bank, cq=cq: lambda eng: eng.tensor_scalar(
                A["scr4k"][:, 512:1024], P[bank][:, 0:512],
                A["bq_c"][:, cq:cq + 1], 0.0, ALU.add, ALU.min))(),
                waits=[(sP, g_done)], inc=sV)
            E.emit("vector", (lambda bank=bank, cq=cq: lambda eng: eng.tensor_scalar(
                A["scr4k"][:, 0:512], P[bank][:, 0:512],
                A["bq_c"][:, cq:cq + 1], 0.0, ALU.add, ALU.max))(), inc=sV)
            v_ab = E.sem_val(sV)
            rot_free[bank] = (sV, v_ab)
            E.emit("scalar", lambda eng: eng.activation(
                A["scr4k"][:, 512:1024], A["scr4k"][:, 512:1024], AF.Exp),
                waits=[(sV, v_ab)], inc=sS, flush=True)
            s_exp = E.sem_val(sS)
            E.emit("vector", (lambda cq=cq: lambda eng: eng.tensor_tensor(
                A["qbf"][:, cq * 512:(cq + 1) * 512], A["scr4k"][:, 0:512],
                A["scr4k"][:, 512:1024], ALU.add))(), waits=[(sS, s_exp)], inc=sV)
            q_done_v = E.sem_val(sV)
        chunk_consumed(q0t, q0s, sV, q_done_v)
        chunk_consumed(q1t, q1s, sV, q_done_v)

        zw = [(sV, q_done_v)]
        if tr_free[6] is not None:
            zw.append(tr_free[6])
        z_p = 0
        for c in range(DC):
            z_p = E.emit("tensor", (lambda c=c: lambda eng: eng.matmul(
                P[6][0:16, 0:512], A["ksml"][:, c * 16:(c + 1) * 16],
                A["qbf"][:, c * 512:(c + 1) * 512],
                start=(c == 0), stop=(c == DC - 1)))(),
                waits=zw if c == 0 else (), inc=sP if c == DC - 1 else None)
        E.emit("vector", lambda eng: eng.tensor_scalar(
            A["zsb"][:], P[6][0:16, 0:512], EPS_Z, 0.0, ALU.add, ALU.add),
            waits=[(sP, z_p)], inc=sV)
        E.emit("vector", lambda eng: eng.reciprocal(A["zsb"][:], A["zsb"][:]), inc=sV)
        zinv_v = E.sem_val(sV)
        tr_free[6] = (sV, zinv_v)

        o_done_v = 0
        for c in range(DC):
            bank = c % 4
            w = [(sV, max(q_done_v, kv_ready))]
            if rot_free[bank] is not None:
                w.append(rot_free[bank])
            E.emit("tensor", (lambda c=c, bank=bank: lambda eng: eng.matmul(
                P[bank][0:64, 0:512], A["kvsb"][0:64, c * 64:(c + 1) * 64],
                A["qbf"][0:64, c * 512:(c + 1) * 512], start=True, stop=True))(),
                waits=w)
            E.emit("tensor", (lambda c=c, bank=bank: lambda eng: eng.matmul(
                P[bank][64:128, 0:512], A["kvsb"][64:128, c * 64:(c + 1) * 64],
                A["qbf"][64:128, c * 512:(c + 1) * 512], start=True, stop=True))(),
                inc=sP)
            zbank = 6 + (c % 2)
            zslot = (c % 2) * 512
            zbw = [(sV, zinv_v), (qSel, sel_ld)]
            if tr_free[zbank] is not None:
                zbw.append(tr_free[zbank])
            zb_p = E.emit("tensor", (lambda c=c, zbank=zbank: lambda eng: eng.matmul(
                P[zbank][:, 0:512], A["selA"][:, c * 128:(c + 1) * 128], A["zsb"][:],
                start=True, stop=True))(), waits=zbw, inc=sP)
            ztile = "zbsb2" if c % 2 == 0 else "zbsbB"
            E.emit("vector", (lambda zbank=zbank, ztile=ztile: lambda eng: eng.tensor_copy(
                A[ztile][:, 0:512], P[zbank][:, 0:512]))(),
                waits=[(sP, zb_p)], inc=sV)
            zb_v = E.sem_val(sV)
            tr_free[zbank] = (sV, zb_v)
            E.emit("vector", (lambda c=c, bank=bank, ztile=ztile: lambda eng: eng.tensor_tensor(
                A["obf"][:, c * 512:(c + 1) * 512], P[bank][:, 0:512],
                A[ztile][:, 0:512], ALU.mult))(), inc=sV)
            o_done_v = E.sem_val(sV)
            rot_free[bank] = (sV, o_done_v)

        o0s, o0t = load_chunk(wb["wo_t"][:, 0:4, :], "wo_t")
        o1s, o1t = load_chunk(wb["wo_t"][:, 4:8, :], "wo_t")
        a_v = 0
        bank_i = 0
        for s in range(NS):
            for g in range(2):
                bank = bank_i % 4
                bank_i += 1
                mats = []
                for c in range(DC):
                    slot = o0s if c < 4 else o1s
                    cc = c % 4
                    mats.append((
                        (lambda c=c, s=s: A["obf"][:, c * 512 + s * 128: c * 512 + s * 128 + 128]),
                        (lambda slot=slot, cc=cc, g=g: A[f"wbuf{slot}"][:, cc * 1024 + g * 512: cc * 1024 + g * 512 + 512]),
                    ))
                g_done = mm_group(bank, mats,
                                  waits=[(sV, o_done_v), chunk_wait(o0t), chunk_wait(o1t)])
                off = s * 1024 + g * 512
                E.emit("vector", (lambda bank=bank, off=off: lambda eng: eng.tensor_tensor(
                    A["res1"][:, off:off + 512], A["res1"][:, off:off + 512],
                    P[bank][:, 0:512], ALU.add))(), waits=[(sP, g_done)], inc=sV)
                a_v = E.sem_val(sV)
                rot_free[bank] = (sV, a_v)
        chunk_consumed(o0t, o0s, sV, a_v)
        chunk_consumed(o1t, o1s, sV, a_v)
        for s in range(NS):
            E.emit("vector", (lambda s=s: lambda eng: eng.tensor_tensor(
                A["res1"][:, s * 1024:(s + 1) * 1024],
                A["res1"][:, s * 1024:(s + 1) * 1024], A["boB"][:], ALU.add))(), inc=sV)
        a_v = E.sem_val(sV)
        if stage == "a":
            return dbg_store(t, A["res1"], a_v)

        E.barrier("scalar", [(sV, a_v)])
        stats1 = ln_stats(A["res1"], A["res2"])
        n1_v = ln_apply(A["res1"], A["res2"], stats1)
        for s in range(NS):
            E.emit("vector", (lambda s=s: lambda eng: eng.tensor_tensor(
                A["res2"][:, s * 1024:(s + 1) * 1024],
                A["res2"][:, s * 1024:(s + 1) * 1024], A["g1B"][:], ALU.mult))(), inc=sV)
            E.emit("vector", (lambda s=s: lambda eng: eng.tensor_tensor(
                A["res2"][:, s * 1024:(s + 1) * 1024],
                A["res2"][:, s * 1024:(s + 1) * 1024], A["be1B"][:], ALU.add))(), inc=sV)
        x1_v = E.sem_val(sV)
        if stage == "x1":
            return dbg_store(t, A["res2"], x1_v)
        fmT1_v = tm_to_fm(A["res2"], "p2", [(sV, x1_v)], A["fmT"])

        flat1 = wb["w1_s"].rearrange("a p d -> p a d")
        g_v = 0
        w1tok = [None, None]
        for cq in range(FC):
            if cq % 4 == 0:
                i = cq // 4
                w1tok = list(load_chunk(flat1[:, i * 4:(i + 1) * 4, :], "w1_s"))
            bank = cq % 4
            slot, tok = w1tok
            ccol = (cq % 4) * 1024
            mats = [((lambda slot=slot, ccol=ccol, c=c: A[f"wbuf{slot}"][:, ccol + c * 128: ccol + (c + 1) * 128]),
                     (lambda c=c: A["fmT"][:, c * 512:(c + 1) * 512]))
                    for c in range(DC)]
            g_done = mm_group(bank, mats, waits=[(sV, fmT1_v), chunk_wait(tok)])
            E.emit("scalar", (lambda bank=bank, cq=cq: lambda eng: eng.activation(
                A["bigA"][:, cq * 512:(cq + 1) * 512], P[bank][:, 0:512], AF.Gelu,
                bias=A["b1_c"][:, cq:cq + 1], scale=1.0))(),
                waits=[(sP, g_done)], inc=sS)
            g_v = E.sem_val(sS)
            rot_free[bank] = (sS, g_v)
            if cq % 4 == 3:
                chunk_consumed(tok, slot, sS, g_v)
        E.emit("scalar", lambda eng: eng.drain(), inc=sS)
        g_v = E.sem_val(sS)

        y_v = 0
        for cq in range(DC):
            slot, tok = load_chunk(wb["w2_s"][cq], "w2_s")
            bank = cq % 4
            mats = [((lambda slot=slot, c=c: A[f"wbuf{slot}"][:, c * 128:(c + 1) * 128]),
                     (lambda c=c: A["bigA"][:, c * 512:(c + 1) * 512]))
                    for c in range(FC)]
            g_done = mm_group(bank, mats, waits=[(sS, g_v), chunk_wait(tok)])
            E.emit("vector", (lambda bank=bank, cq=cq: lambda eng: eng.tensor_scalar(
                A["scrF"][:, cq * 512:(cq + 1) * 512], P[bank][:, 0:512],
                A["b2_c"][:, cq:cq + 1], 0.0, ALU.add, ALU.add))(),
                waits=[(sP, g_done)], inc=sV)
            y_v = E.sem_val(sV)
            rot_free[bank] = (sV, y_v)
            chunk_consumed(tok, slot, sV, y_v)

        if stage == "yfm":
            return dbg_store(t, A["scrF"], y_v)
        r1_v = fm_to_tm(A["scrF"], "p2", [(sV, y_v)], A["res2"], add=True)
        if stage == "r1":
            return dbg_store(t, A["res2"], r1_v)

        E.barrier("scalar", [(sV, r1_v)])
        stats2 = ln_stats(A["res2"], A["res1"])
        n2_v = ln_apply(A["res2"], A["res2"], stats2)
        for s in range(NS):
            E.emit("vector", (lambda s=s: lambda eng: eng.tensor_tensor(
                A["res2"][:, s * 1024:(s + 1) * 1024],
                A["res2"][:, s * 1024:(s + 1) * 1024], A["g2B"][:], ALU.mult))(), inc=sV)
            E.emit("vector", (lambda s=s: lambda eng: eng.tensor_tensor(
                A["res2"][:, s * 1024:(s + 1) * 1024],
                A["res2"][:, s * 1024:(s + 1) * 1024], A["be2B"][:], ALU.add))(), inc=sV)
        blk_v = E.sem_val(sV)
        E.barrier("scalar", [(sV, blk_v)])
        stats3 = ln_stats(A["res2"], A["res1"])
        v_v = ln_apply(A["res2"], A["res2"], stats3)
        for s in range(NS):
            E.emit("vector", (lambda s=s: lambda eng: eng.tensor_tensor(
                A["res1"][:, s * 1024:(s + 1) * 1024],
                A["xbuf"][:, s * 1024:(s + 1) * 1024],
                A["res2"][:, s * 1024:(s + 1) * 1024], ALU.add))(), inc=sV)
        x2_v = E.sem_val(sV)
        if stage == "x2":
            return dbg_store(t, A["res1"], x2_v)

        E.barrier("scalar", [(sV, x2_v)])
        stats4 = ln_stats(A["res1"], A["res2"])
        ln2_v = ln_apply(A["res1"], A["res2"], stats4)
        if stage == "ln2t":
            return dbg_store(t, A["res2"], ln2_v)
        fmT2_v = tm_to_fm(A["res2"], "p2", [(sV, ln2_v)], A["fmT"])

        early_v = None
        if t + 1 < n_tiles:
            E.emit("scalar", (lambda t=t: lambda eng: eng.dma_start(
                out=xbuf_3d[:], in_=x_tiled[t + 1]))(),
                waits=[(sV, x2_v)], inc=qX, amt=16)
            ex_ld = E.sem_val(qX)
            E.barrier("vector", [(qX, ex_ld)])
            E.barrier("scalar", [(qX, ex_ld)])
            early_v = ln_stats(A["xbuf"], A["scrF"], base=16)
            state["early"] = (early_v, (qX, ex_ld))

        flatw1 = wb["fw1_s"].rearrange("a p d -> p a d")
        flatw2 = wb["fw2_s"].rearrange("a p d -> p a d")
        t1_v = 0
        gtoks = [None] * 4
        for cq in range(FC):
            if cq % 4 == 0:
                i = cq // 4
                gtoks[0], gtoks[1] = load_chunk(flatw1[:, i * 4:(i + 1) * 4, :], "fw1_s")
                gtoks[2], gtoks[3] = load_chunk(flatw2[:, i * 4:(i + 1) * 4, :], "fw2_s")
            bankA = (2 * cq) % 4
            bankB = (2 * cq + 1) % 4
            ccol = (cq % 4) * 1024
            mats1 = [((lambda slot=gtoks[0], ccol=ccol, c=c: A[f"wbuf{slot}"][:, ccol + c * 128: ccol + (c + 1) * 128]),
                      (lambda c=c: A["fmT"][:, c * 512:(c + 1) * 512]))
                     for c in range(DC)]
            mats2 = [((lambda slot=gtoks[2], ccol=ccol, c=c: A[f"wbuf{slot}"][:, ccol + c * 128: ccol + (c + 1) * 128]),
                      (lambda c=c: A["fmT"][:, c * 512:(c + 1) * 512]))
                     for c in range(DC)]
            gd1 = mm_group(bankA, mats1, waits=[(sV, fmT2_v), chunk_wait(gtoks[1])])
            gd2 = mm_group(bankB, mats2, waits=[(sV, fmT2_v), chunk_wait(gtoks[3])])
            E.emit("scalar", (lambda bankA=bankA, cq=cq: lambda eng: eng.activation(
                A["scr4k"][:, 0:512], P[bankA][:, 0:512], AF.Silu,
                bias=0.0, scale=A["s1_c"][:, cq:cq + 1]))(),
                waits=[(sP, gd1), (sV, t1_v)], inc=sS, flush=True)
            sil_v = E.sem_val(sS)
            rot_free[bankA] = (sS, sil_v)
            E.emit("vector", (lambda bankB=bankB, cq=cq: lambda eng: eng.scalar_tensor_tensor(
                A["bigA"][:, cq * 512:(cq + 1) * 512], P[bankB][:, 0:512],
                A["s2_c"][:, cq:cq + 1], A["scr4k"][:, 0:512], ALU.mult, ALU.mult))(),
                waits=[(sP, gd2), (sS, sil_v)], inc=sV)
            t1_v = E.sem_val(sV)
            rot_free[bankB] = (sV, t1_v)
            if cq % 4 == 3:
                chunk_consumed(gtoks[1], gtoks[0], sV, t1_v)
                chunk_consumed(gtoks[3], gtoks[2], sV, t1_v)

        def big_ff(wname, src, dst, scol, prev):
            last_s = 0
            for cq in range(FC):
                slot, tok = load_chunk(wb[wname][cq], wname)
                bank = cq % 6
                mats = [((lambda slot=slot, c=c: A[f"wbuf{slot}"][:, c * 128:(c + 1) * 128]),
                         (lambda c=c, src=src: src[:, c * 512:(c + 1) * 512]))
                        for c in range(FC)]
                g_done = mm_group(bank, mats, waits=[prev, chunk_wait(tok)])
                E.emit("scalar", (lambda bank=bank, cq=cq, dst=dst, scol=scol:
                                  lambda eng: eng.activation(
                    dst[:, cq * 512:(cq + 1) * 512], P[bank][:, 0:512], AF.Gelu,
                    bias=0.0, scale=A[scol][:, cq:cq + 1]))(),
                    waits=[(sP, g_done)], inc=sS)
                last_s = E.sem_val(sS)
                rot_free[bank] = (sS, last_s)
                chunk_consumed(tok, slot, sS, last_s)
            E.emit("scalar", lambda eng: eng.drain(), inc=sS)
            return E.sem_val(sS)

        t2_v = big_ff("h1_s", A["bigA"], A["bigB"], "sh1_c", (sV, t1_v))
        t3_v = big_ff("h2_s", A["bigB"], A["bigA"], "sh2_c", (sS, t2_v))

        f_v = 0
        for cq in range(DC):
            slot, tok = load_chunk(wb["w3_s"][cq], "w3_s")
            bank = cq % 4
            mats = [((lambda slot=slot, c=c: A[f"wbuf{slot}"][:, c * 128:(c + 1) * 128]),
                     (lambda c=c: A["bigA"][:, c * 512:(c + 1) * 512]))
                    for c in range(FC)]
            g_done = mm_group(bank, mats, waits=[(sS, t3_v), chunk_wait(tok)])
            wv = [(sP, g_done)]
            if cq == 0 and early_v is not None:
                wv.append((sV, early_v))
            E.emit("vector", (lambda bank=bank, cq=cq: lambda eng: eng.tensor_scalar(
                A["scrF"][:, cq * 512:(cq + 1) * 512], P[bank][:, 0:512],
                A["s3_c"][:, cq:cq + 1], 0.0, ALU.mult, ALU.add))(),
                waits=wv, inc=sV)
            f_v = E.sem_val(sV)
            rot_free[bank] = (sV, f_v)
            chunk_consumed(tok, slot, sV, f_v)
        state["sel_gate"] = (sV, f_v)

        if stage == "ffm":
            return dbg_store(t, A["scrF"], f_v)
        ftm_v = fm_to_tm(A["scrF"], "p2", [(sV, f_v)], A["res2"], add=False)
        E.barrier("scalar", [(sV, ftm_v)])
        stats5 = ln_stats(A["res2"], A["scrF"])
        nf_v = ln_apply(A["res2"], A["res2"], stats5)
        for s in range(NS):
            E.emit("vector", (lambda s=s: lambda eng: eng.tensor_tensor(
                A["res1"][:, s * 1024:(s + 1) * 1024],
                A["res1"][:, s * 1024:(s + 1) * 1024],
                A["res2"][:, s * 1024:(s + 1) * 1024], ALU.add))(), inc=sV)
        out_v = E.sem_val(sV)

        E.emit("scalar", (lambda t=t: lambda eng: eng.dma_start(
            out=out_tiled[t], in_=A["res1"].rearrange("p (s d) -> p s d", d=D)))(),
            waits=[(sV, out_v)], inc=qX, amt=16)
        state["store_done"] = (qX, E.sem_val(qX))
        state["xbuf_free"] = (sV, out_v)

    for t in range(n_tiles):
        pass2_tile(t)

    E.barrier("scalar", [(qX, E.sem_val(qX))])
    return finish(nc, E, stack)


def finish(nc, E, stack):
    with nc.Block() as blk:
        def make_body(eng_name):
            def body(eng):
                for fn, waits, inc in E.ev[eng_name]:
                    for sem, val in waits:
                        eng.wait_ge(sem, val)
                    if fn is None:
                        assert inc is None
                        continue
                    inst = fn(eng)
                    if inc is not None:
                        inst.then_inc(inc[0], inc[1])
            return body
        blk.sync(make_body("sync"))
        blk.gpsimd(make_body("gpsimd"))
        blk.scalar(make_body("scalar"))
        blk.vector(make_body("vector"))
        blk.tensor(make_body("tensor"))
    stack.close()
    return nc


def _prep_inputs(ins):
    f32 = np.float32

    def fm_swizzle(w_in_out, co_n, ci_n):
        a = np.asarray(w_in_out, f32).reshape(ci_n, 128, co_n, 128)
        return np.ascontiguousarray(a.transpose(2, 1, 0, 3).reshape(co_n, 128, ci_n * 128))

    def tm_swizzle(w_in_out):
        a = np.asarray(w_in_out, f32).reshape(8, 128, -1)
        return np.ascontiguousarray(a.transpose(1, 0, 2))

    def colmat(v, n):
        return np.ascontiguousarray(np.asarray(v, f32).reshape(n, 128).T)

    d = {}
    d["wk_t"] = tm_swizzle(ins["L0_attn_Wk"])
    d["wv_t"] = tm_swizzle(ins["L0_attn_Wv"])
    d["wo_t"] = tm_swizzle(ins["L0_attn_Wo"])
    d["wq_s"] = fm_swizzle(ins["L0_attn_Wq"], 8, 8)
    d["w1_s"] = fm_swizzle(ins["L0_attn_W1"], 32, 8)
    d["w2_s"] = fm_swizzle(ins["L0_attn_W2"], 8, 32)
    d["fw1_s"] = fm_swizzle(np.asarray(ins["L0_ffn_w1_w"], f32).T, 32, 8)
    d["fw2_s"] = fm_swizzle(np.asarray(ins["L0_ffn_w2_w"], f32).T, 32, 8)
    d["h1_s"] = fm_swizzle(np.asarray(ins["L0_ffn_h1_w"], f32).T, 32, 32)
    d["h2_s"] = fm_swizzle(np.asarray(ins["L0_ffn_h2_w"], f32).T, 32, 32)
    d["w3_s"] = fm_swizzle(np.asarray(ins["L0_ffn_w3_w"], f32).T, 8, 32)
    for k, src in [("bk", "L0_attn_bk"), ("bv", "L0_attn_bv"), ("bo", "L0_attn_bo"),
                   ("g1", "L0_attn_g1"), ("be1", "L0_attn_be1"),
                   ("g2", "L0_attn_g2"), ("be2", "L0_attn_be2")]:
        d[k] = np.ascontiguousarray(np.asarray(ins[src], f32))
    d["bq_c"] = colmat(ins["L0_attn_bq"], 8)
    d["b1_c"] = colmat(ins["L0_attn_b1"], 32)
    d["b2_c"] = colmat(ins["L0_attn_b2"], 8)
    d["s1_c"] = colmat(ins["L0_ffn_w1_s"], 32)
    d["s2_c"] = colmat(ins["L0_ffn_w2_s"], 32)
    d["sh1_c"] = colmat(ins["L0_ffn_h1_s"], 32)
    d["sh2_c"] = colmat(ins["L0_ffn_h2_s"], 32)
    d["s3_c"] = colmat(ins["L0_ffn_w3_s"], 8)
    d["id_f32"] = np.eye(128, dtype=f32)
    sel = np.zeros((16, 1024), f32)
    for c in range(8):
        sel[2 * c, c * 128:c * 128 + 64] = 1.0
        sel[2 * c + 1, c * 128 + 64:c * 128 + 128] = 1.0
    d["sel_f32"] = sel
    return d


def kernel(**inputs):
    shared = _prep_inputs(inputs)
    x = np.asarray(inputs["L0_attn_src"], np.float32)
    nc = build("full")
    in_maps = []
    for b in range(8):
        m = dict(shared)
        m["x"] = np.ascontiguousarray(x[b])
        in_maps.append(m)
    res = run_bass_kernel_spmd(nc, in_maps, core_ids=list(range(8)))
    out = np.stack([np.asarray(r["out"]) for r in res.results], axis=0)
    return out.astype(np.float32)


if __name__ == "__main__":
    print("building program...")
    nc = build("full")
    print("n_instructions:", nc.n_instructions())


# version 08
# speedup vs baseline: 1.0321x; 1.0321x over previous
import contextlib
import numpy as np
import concourse.bass as bass
import concourse.mybir as mybir
from concourse.bass_utils import run_bass_kernel_spmd

F32 = mybir.dt.float32
BF16 = mybir.dt.bfloat16
AF = mybir.ActivationFunctionType
ALU = mybir.AluOpType
AX = mybir.AxisListType

S, D, H, DH, FF = 4096, 1024, 16, 64, 4096
T = 512
NT = S // T
NS = T // 128
DC = D // 128
FC = FF // 128
EPS_LN = 1e-5
EPS_Z = 1e-6
KB = 1024

ARENA = 16512
OFF = dict(xbuf=ARENA, res1=ARENA + 16 * KB, res2=ARENA + 32 * KB,
           fmT=ARENA + 48 * KB, scrF=ARENA + 56 * KB, bigA=ARENA + 72 * KB,
           bigB=ARENA + 104 * KB, wbuf=ARENA + 136 * KB, consts=ARENA + 168 * KB)


def _roundup(x, a):
    return (x + a - 1) // a * a


class Emitter:
    def __init__(self):
        self.ev = {k: [] for k in ("sync", "gpsimd", "scalar", "vector", "tensor")}
        self.ctr = {}
        self.sems = {}
        self.bounce_cache = {}

    def sem_val(self, sem):
        return self.ctr.get(sem.name, 0)

    def bounce(self, sem, val):
        key = (sem.name, val)
        if key in self.bounce_cache:
            return self.bounce_cache[key]
        via = "scalar" if sem.name == "sV" else "vector"
        via_sem = self.sems["sS"] if via == "scalar" else self.sems["sV"]
        self.ev[via].append((None, [(sem, val)], None))
        self.ctr[via_sem.name] = self.ctr.get(via_sem.name, 0) + 1
        self.ev[via].append((lambda eng_: eng_.drain(), [], (via_sem, 1)))
        out = (via_sem, self.ctr[via_sem.name])
        self.bounce_cache[key] = out
        return out

    def emit(self, eng, fn, waits=(), inc=None, amt=1, flush=False):
        w = []
        for (sem, val) in waits:
            if val is None or val <= 0:
                continue
            assert val <= self.ctr.get(sem.name, 0), \
                f"wait {sem.name}>={val} but only {self.ctr.get(sem.name, 0)} emitted"
            w.append((sem, val))
        if inc is not None and amt == 1 and eng in ("vector", "scalar", "gpsimd"):
            self.ev[eng].append((fn, w, None))
            self.ctr[inc.name] = self.ctr.get(inc.name, 0) + amt
            self.ev[eng].append((lambda eng_: eng_.drain(), [], (inc, amt)))
            return self.ctr.get(inc.name, 0)
        incv = None
        if inc is not None:
            self.ctr[inc.name] = self.ctr.get(inc.name, 0) + amt
            incv = (inc, amt)
        self.ev[eng].append((fn, w, incv))
        return self.ctr.get(inc.name, 0) if inc is not None else None

    def barrier(self, eng, waits):
        self.emit(eng, None, waits=waits)


def build(stage="full", n_tiles=NT):
    nc = bass.Bass("TRN2", target_bir_lowering=False, debug=False)

    x_h = nc.dram_tensor("x", [S, D], F32, kind="ExternalInput").ap()
    out_h = nc.dram_tensor("out", [S, D], F32, kind="ExternalOutput").ap()

    wf = {}
    for name, shape in [
        ("wk_t", [128, DC, D]), ("wv_t", [128, DC, D]), ("wo_t", [128, DC, D]),
        ("wq_s", [DC, 128, D]),
        ("w1_s", [FC, 128, D]), ("w2_s", [DC, 128, FF]),
        ("fw1_s", [FC, 128, D]), ("fw2_s", [FC, 128, D]),
        ("h1_s", [FC, 128, FF]), ("h2_s", [FC, 128, FF]),
        ("w3_s", [DC, 128, FF]),
    ]:
        wf[name] = nc.dram_tensor(name, shape, F32, kind="ExternalInput").ap()
    wb = {name: nc.dram_tensor(name + "_b", list(ap.shape), BF16, kind="Internal").ap()
          for name, ap in wf.items()}

    vecs = {name: nc.dram_tensor(name, [D], F32, kind="ExternalInput").ap()
            for name in ("bk", "bv", "bo", "g1", "be1", "g2", "be2")}
    cols_h = {}
    for name, w in [("bq_c", DC), ("b2_c", DC), ("s3_c", DC), ("b1_c", FC),
                    ("s1_c", FC), ("s2_c", FC), ("sh1_c", FC), ("sh2_c", FC)]:
        cols_h[name] = nc.dram_tensor(name, [128, w], F32, kind="ExternalInput").ap()
    id_h = nc.dram_tensor("id_f32", [128, 128], F32, kind="ExternalInput").ap()
    sel_h = nc.dram_tensor("sel_f32", [16, 1024], F32, kind="ExternalInput").ap()

    A = {}

    def sba(name, shape, dtype, off):
        A[name] = nc.alloc_sbuf_tensor_at(name, shape, dtype, offset=off).ap()

    sba("xbuf", [128, 4096], F32, OFF["xbuf"])
    sba("res1", [128, 4096], F32, OFF["res1"])
    sba("res2", [128, 4096], F32, OFF["res2"])
    sba("fmT", [128, 4096], BF16, OFF["fmT"])
    sba("kbf", [128, 4096], BF16, OFF["scrF"])
    sba("vbf", [128, 4096], BF16, OFF["scrF"] + 8 * KB)
    sba("qbf", [128, 4096], BF16, OFF["scrF"])
    sba("obf", [128, 4096], BF16, OFF["scrF"] + 8 * KB)
    sba("scrF", [128, 4096], F32, OFF["scrF"])
    sba("bigA", [128, 16384], BF16, OFF["bigA"])
    sba("bigB", [128, 16384], BF16, OFF["bigB"])
    sba("selA", [16, 1024], F32, OFF["bigA"] + 28 * KB)
    sba("zbsbB", [128, 512], F32, OFF["bigB"] + 30 * KB)
    for i in range(4):
        sba(f"wbuf{i}", [128, 4096], BF16, OFF["wbuf"] + i * 8 * KB)

    co = OFF["consts"]

    def calloc(name, shape, dtype):
        nonlocal co
        n = int(np.prod(shape[1:]))
        sba(name, shape, dtype, co)
        co += _roundup(n * (2 if dtype == BF16 else 4), 32)

    for nm in ("bkB", "bvB", "boB", "g1B", "be1B", "g2B", "be2B"):
        calloc(nm, [128, 1024], F32)
    for nm, w in [("bq_c", DC), ("b2_c", DC), ("s3_c", DC), ("b1_c", FC),
                  ("s1_c", FC), ("s2_c", FC), ("sh1_c", FC), ("sh2_c", FC)]:
        calloc(nm, [128, w], F32)
    calloc("idf", [128, 128], F32)
    calloc("kvsb", [128, 512], BF16)
    calloc("ksml", [128, 128], BF16)
    calloc("zsb", [16, 512], F32)
    calloc("zbsb2", [128, 512], F32)
    calloc("ones", [128, 1], BF16)
    calloc("stats", [128, 32], F32)
    calloc("pad8", [128, 8], F32)
    calloc("scr4k", [128, 1024], F32)
    assert co <= 229344, f"consts overflow: {co}"

    P = [nc.alloc_psum_tensor(f"p{i}", [128, 512], F32).ap() for i in range(8)]

    E = Emitter()
    stack = contextlib.ExitStack()
    qW = [stack.enter_context(nc.semaphore(f"qW{i}")) for i in range(4)]
    qK = stack.enter_context(nc.semaphore("qK"))
    qSel = stack.enter_context(nc.semaphore("qSel"))
    qX = stack.enter_context(nc.semaphore("qX"))
    qC = stack.enter_context(nc.semaphore("qC"))
    sP = stack.enter_context(nc.semaphore("sP"))
    sV = stack.enter_context(nc.semaphore("sV"))
    sS = stack.enter_context(nc.semaphore("sS"))
    E.sems = {"sV": sV, "sS": sS}

    for nm, dst in [("bk", "bkB"), ("bv", "bvB"), ("bo", "boB"), ("g1", "g1B"),
                    ("be1", "be1B"), ("g2", "g2B"), ("be2", "be2B")]:
        E.emit("sync", (lambda d=A[dst], s=vecs[nm]: lambda eng: eng.dma_start(
            out=d[:], in_=s.unsqueeze(0).partition_broadcast(128)))(), inc=qK, amt=16)
    for nm in ("bq_c", "b2_c", "s3_c", "b1_c", "s1_c", "s2_c", "sh1_c", "sh2_c"):
        E.emit("sync", (lambda d=A[nm], s=cols_h[nm]: lambda eng: eng.dma_start(
            out=d[:], in_=s))(), inc=qK, amt=16)
    E.emit("sync", lambda eng: eng.dma_start(out=A["idf"][:], in_=id_h), inc=qK, amt=16)
    consts_loaded = E.sem_val(qK)

    cast_done = {}
    for name in ("wk_t", "wv_t", "wq_s", "wo_t", "w1_s", "w2_s",
                 "fw1_s", "fw2_s", "h1_s", "h2_s", "w3_s"):
        src, dst = wf[name], wb[name]
        n0 = src.shape[0]
        pieces = 4 if int(np.prod(src.shape)) * 4 > 20e6 else 1
        step = n0 // pieces
        for i in range(pieces):
            prev = E.sem_val(qC)
            E.emit("gpsimd", (lambda d=dst, s=src, a=i * step, b=(i + 1) * step:
                              lambda eng: eng.dma_start(out=d[a:b], in_=s[a:b]))(),
                   waits=[(qC, prev)], inc=qC, amt=16)
        cast_done[name] = E.sem_val(qC)

    ring = {"idx": 0, "loadedv": {}, "freed": [None] * 4}

    def load_chunk(dram_ap, matrix):
        slot = ring["idx"] % 4
        waits = [(qC, cast_done[matrix])]
        if ring["freed"][slot] is not None:
            waits.append(ring["freed"][slot])
        if ring.get("extra"):
            waits.append(ring["extra"])
        if len(dram_ap.shape) == 3:
            dview = A[f"wbuf{slot}"].rearrange("p (a d) -> p a d", d=dram_ap.shape[2])
        else:
            dview = A[f"wbuf{slot}"][:]
        E.emit("sync", (lambda d=dview, s=dram_ap:
                        lambda eng: eng.dma_start(out=d, in_=s))(),
               waits=waits, inc=qW[slot], amt=16)
        tok = ring["idx"]
        ring["loadedv"][tok] = (qW[slot], E.sem_val(qW[slot]))
        ring["idx"] += 1
        return slot, tok

    def chunk_wait(tok):
        return ring["loadedv"][tok]

    def chunk_consumed(tok, slot, sem, val):
        ring["freed"][slot] = (sem, val)

    def ln_stats(src, scratch, src_ready=None, base=0):
        st = A["stats"]
        b = base
        for s in range(NS):
            ws = list(src_ready or []) if s == 0 else []
            E.emit("scalar", (lambda s=s, scr=scratch: lambda eng: eng.activation(
                scr[:, s * 1024:(s + 1) * 1024], src[:, s * 1024:(s + 1) * 1024],
                AF.Square))(), waits=ws,
                inc=sS if s == NS - 1 else None, flush=(s == NS - 1))
        sq_done = E.sem_val(sS)
        for s in range(NS):
            E.emit("vector", (lambda s=s, b=b: lambda eng: eng.reduce_sum(
                st[:, b + s:b + s + 1], src[:, s * 1024:(s + 1) * 1024], axis=AX.X))(),
                inc=sV)
        for s in range(NS):
            E.emit("vector", (lambda s=s, b=b, scr=scratch: lambda eng: eng.reduce_sum(
                st[:, b + 4 + s:b + 5 + s], scr[:, s * 1024:(s + 1) * 1024], axis=AX.X))(),
                waits=[(sS, sq_done)] if s == 0 else (), inc=sV)
        for j in range(3):
            E.emit("vector", (lambda j=j: lambda eng: eng.memset(A["pad8"][:], float(j)))())
        E.emit("vector", (lambda b=b: lambda eng: eng.tensor_scalar(
            st[:, b:b + 4], st[:, b:b + 4], 1.0 / D, 0.0, ALU.mult, ALU.add))(), inc=sV)
        E.emit("vector", (lambda b=b: lambda eng: eng.tensor_scalar(
            st[:, b + 4:b + 8], st[:, b + 4:b + 8], 1.0 / D, EPS_LN, ALU.mult, ALU.add))(), inc=sV)
        E.emit("vector", (lambda b=b: lambda eng: eng.tensor_tensor(
            st[:, b + 8:b + 12], st[:, b:b + 4], st[:, b:b + 4], ALU.mult))(), inc=sV)
        E.emit("vector", (lambda b=b: lambda eng: eng.tensor_tensor(
            st[:, b + 4:b + 8], st[:, b + 4:b + 8], st[:, b + 8:b + 12], ALU.subtract))(), inc=sV)
        v_var = E.sem_val(sV)
        E.emit("scalar", (lambda b=b: lambda eng: eng.activation(
            st[:, b + 4:b + 8], st[:, b + 4:b + 8], AF.Sqrt))(),
            waits=[(sV, v_var)], inc=sS, flush=True)
        s_sqrt = E.sem_val(sS)
        E.emit("vector", (lambda b=b: lambda eng: eng.reciprocal(
            st[:, b + 4:b + 8], st[:, b + 4:b + 8]))(),
               waits=[(sS, s_sqrt)], inc=sV)
        return E.sem_val(sV)

    def ln_apply(src, dst, stats_ready, base=0, extra_waits=()):
        st = A["stats"]
        for s in range(NS):
            w = ([(sV, stats_ready)] + list(extra_waits)) if s == 0 else []
            E.emit("vector", (lambda s=s, b=base: lambda eng: eng.tensor_scalar(
                dst[:, s * 1024:(s + 1) * 1024], src[:, s * 1024:(s + 1) * 1024],
                st[:, b + s:b + s + 1], st[:, b + 4 + s:b + 5 + s],
                ALU.subtract, ALU.mult))(), waits=w, inc=sV)
        return E.sem_val(sV)

    tr_banks = {"p1": (4, 5), "p2": (6, 7)}
    tr_free = {4: None, 5: None, 6: None, 7: None}
    rot_free = {i: None for i in range(6)}

    def transpose_sweep(src, which, ready_w, evac):
        b0, b1 = tr_banks[which]
        last_v = 0
        for s in range(NS):
            for half in range(2):
                bank = b0 if half == 0 else b1
                waits = list(ready_w)
                if tr_free[bank] is not None:
                    waits.append(tr_free[bank])
                for j in range(4):
                    c = half * 4 + j
                    E.emit("tensor", (lambda s=s, c=c, j=j, bank=bank, src=src:
                                      lambda eng: eng.transpose(
                        P[bank][:, j * 128:(j + 1) * 128],
                        src[:, ln_tr_in(s, c)], A["idf"][:]))(),
                        waits=waits if j == 0 else (),
                        inc=sP if j == 3 else None)
                p_done = E.sem_val(sP)
                last_v = evac(s, half, bank, p_done)
                tr_free[bank] = (sV, last_v)
        return last_v

    def ln_tr_in(s, c):
        return slice(s * 1024 + c * 128, s * 1024 + (c + 1) * 128)

    def tm_to_fm(src, which, ready_w, dst):
        def evac(s, half, bank, p_done):
            E.emit("vector", (lambda s=s, half=half, bank=bank, d=dst:
                              lambda eng: eng.tensor_copy(
                d.rearrange("p (c x) -> p c x", x=512)[:, half * 4:half * 4 + 4,
                                                       s * 128:s * 128 + 128],
                P[bank].rearrange("p (j x) -> p j x", x=128)))(),
                waits=[(sP, p_done)], inc=sV)
            return E.sem_val(sV)
        return transpose_sweep(src, which, ready_w, evac)

    def fm_to_tm(src_fm, which, ready_w, dst_tm, add):
        b0, b1 = tr_banks[which]
        last_v = 0
        for s in range(NS):
            for half in range(2):
                bank = b0 if half == 0 else b1
                waits = list(ready_w)
                if tr_free[bank] is not None:
                    waits.append(tr_free[bank])
                for j in range(4):
                    c = half * 4 + j
                    E.emit("tensor", (lambda s=s, c=c, j=j, bank=bank, src=src_fm:
                                      lambda eng: eng.transpose(
                        P[bank][:, j * 128:(j + 1) * 128],
                        src[:, c * 512 + s * 128: c * 512 + s * 128 + 128],
                        A["idf"][:]))(),
                        waits=waits if j == 0 else (),
                        inc=sP if j == 3 else None)
                p_done = E.sem_val(sP)
                dview = dst_tm.rearrange("p (s c x) -> p s c x", s=NS, x=128)
                if add:
                    E.emit("vector", (lambda s=s, half=half, bank=bank, dv=dview:
                                      lambda eng: eng.tensor_tensor(
                        dv[:, s, half * 4:half * 4 + 4, :],
                        dv[:, s, half * 4:half * 4 + 4, :],
                        P[bank].rearrange("p (j x) -> p j x", x=128), ALU.add))(),
                        waits=[(sP, p_done)], inc=sV)
                else:
                    E.emit("vector", (lambda s=s, half=half, bank=bank, dv=dview:
                                      lambda eng: eng.tensor_copy(
                        dv[:, s, half * 4:half * 4 + 4, :],
                        P[bank].rearrange("p (j x) -> p j x", x=128)))(),
                        waits=[(sP, p_done)], inc=sV)
                last_v = E.sem_val(sV)
                tr_free[bank] = (sV, last_v)
        return last_v

    def mm_group(bank, matmuls, waits=()):
        n = len(matmuls)
        for i, fn in enumerate(matmuls):
            w = []
            if i == 0:
                if rot_free[bank] is not None:
                    w.append(rot_free[bank])
                w += list(waits)
            E.emit("tensor", (lambda fn=fn, i=i, bank=bank, n=n:
                              lambda eng: eng.matmul(
                P[bank][:, 0:512], fn[0](), fn[1](),
                start=(i == 0), stop=(i == n - 1)))(),
                waits=w, inc=sP if i == n - 1 else None)
        return E.sem_val(sP)

    x_tiled = x_h.rearrange("(t s p) d -> t p s d", p=128, s=NS)
    out_tiled = out_h.rearrange("(t s p) d -> t p s d", p=128, s=NS)
    xbuf_3d = A["xbuf"].rearrange("p (s d) -> p s d", d=D)

    state = {"xbuf_free": None, "store_done": None, "kv_first": True,
             "ks_first": True}

    def pass1_tile(t):
        waits = []
        if state["xbuf_free"] is not None:
            waits.append(state["xbuf_free"])
        E.emit("scalar", (lambda t=t: lambda eng: eng.dma_start(
            out=xbuf_3d[:], in_=x_tiled[t]))(), waits=waits, inc=qX, amt=16)
        x_ld = E.sem_val(qX)
        E.barrier("vector", [(qX, x_ld), (qK, consts_loaded if t == 0 else 0)])
        E.barrier("scalar", [(qX, x_ld)])
        stats_v = ln_stats(A["xbuf"], A["res2"])
        u_v = ln_apply(A["xbuf"], A["res1"], stats_v)
        state["xbuf_free"] = (sV, u_v)
        fmT_v = tm_to_fm(A["res1"], "p1", [(sV, u_v)], A["fmT"])

        def proj_tm(wname, dst_bf, is_k):
            s0, t0c = load_chunk(wb[wname][:, 0:4, :], wname)
            s1, t1c = load_chunk(wb[wname][:, 4:8, :], wname)
            lastV = 0
            bank_i = 0
            for s in range(NS):
                for g in range(2):
                    bank = bank_i % 4
                    bank_i += 1
                    mats = []
                    for c in range(DC):
                        slot = s0 if c < 4 else s1
                        cc = c % 4
                        mats.append((
                            (lambda c=c, s=s: A["fmT"][:, c * 512 + s * 128: c * 512 + s * 128 + 128]),
                            (lambda slot=slot, cc=cc, g=g: A[f"wbuf{slot}"][:, cc * 1024 + g * 512: cc * 1024 + g * 512 + 512]),
                        ))
                    g_done = mm_group(bank, mats,
                                      waits=[(sV, fmT_v), chunk_wait(t0c), chunk_wait(t1c)])
                    off = s * 1024 + g * 512
                    if is_k:
                        E.emit("vector", (lambda bank=bank, g=g: lambda eng: eng.tensor_tensor(
                            A["scr4k"][:, 0:512], P[bank][:, 0:512],
                            A["bkB"][:, g * 512:(g + 1) * 512], ALU.add))(),
                            waits=[(sP, g_done)], inc=sV)
                        v_tt = E.sem_val(sV)
                        rot_free[bank] = (sV, v_tt)
                        E.emit("vector", lambda eng: eng.tensor_scalar(
                            A["scr4k"][:, 512:1024], A["scr4k"][:, 0:512],
                            0.0, 0.0, ALU.min, ALU.add), inc=sV)
                        v_min = E.sem_val(sV)
                        E.emit("scalar", lambda eng: eng.activation(
                            A["scr4k"][:, 512:1024], A["scr4k"][:, 512:1024], AF.Exp),
                            waits=[(sV, v_min)], inc=sS, flush=True)
                        s_exp = E.sem_val(sS)
                        E.emit("vector", (lambda off=off, dst=dst_bf: lambda eng: eng.scalar_tensor_tensor(
                            dst[:, off:off + 512], A["scr4k"][:, 0:512], 0.0,
                            A["scr4k"][:, 512:1024], ALU.max, ALU.add))(),
                            waits=[(sS, s_exp)], inc=sV)
                        lastV = E.sem_val(sV)
                    else:
                        E.emit("vector", (lambda bank=bank, g=g, off=off, dst=dst_bf:
                                          lambda eng: eng.tensor_tensor(
                            dst[:, off:off + 512], P[bank][:, 0:512],
                            A["bvB"][:, g * 512:(g + 1) * 512], ALU.add))(),
                            waits=[(sP, g_done)], inc=sV)
                        lastV = E.sem_val(sV)
                        rot_free[bank] = (sV, lastV)
            chunk_consumed(t0c, s0, sV, lastV)
            chunk_consumed(t1c, s1, sV, lastV)
            return lastV

        k_done = proj_tm("wk_t", A["kbf"], True)
        v_done = proj_tm("wv_t", A["vbf"], False)

        if t == 0:
            E.emit("vector", lambda eng: eng.memset(A["ones"][:], 1.0), inc=sV)
        ones_v = E.sem_val(sV)

        kv_p = 0
        for s in range(NS):
            for h in range(H):
                pb = (h % 2) * 64
                c = h // 2
                w = [(sV, max(k_done, v_done, ones_v))] if (s == 0 and h == 0) else []
                last = (t == n_tiles - 1 and s == NS - 1 and h == H - 1)
                E.emit("tensor", (lambda s=s, h=h, pb=pb, c=c, last=last: lambda eng: eng.matmul(
                    P[6][pb:pb + 64, c * 64:(c + 1) * 64],
                    A["kbf"][:, s * 1024 + h * 64: s * 1024 + (h + 1) * 64],
                    A["vbf"][:, s * 1024 + h * 64: s * 1024 + (h + 1) * 64],
                    start=state["kv_first"], stop=last))(), waits=w)
                state["kv_first"] = False
                kv_p = E.emit("tensor", (lambda s=s, h=h, pb=pb, c=c, last=last: lambda eng: eng.matmul(
                    P[7][pb:pb + 64, c:c + 1],
                    A["kbf"][:, s * 1024 + h * 64: s * 1024 + (h + 1) * 64],
                    A["ones"][:, 0:1],
                    start=state["ks_first"], stop=last))(),
                    inc=sP if last else None)
                state["ks_first"] = False
        return kv_p

    for t in range(n_tiles):
        kv_p = pass1_tile(t)

    E.barrier("scalar", [(sP, kv_p)])
    E.emit("scalar", lambda eng: eng.drain(), inc=sS)
    kv_sS = E.sem_val(sS)
    E.emit("vector", lambda eng: eng.tensor_copy(A["kvsb"][:], P[6][:]),
           waits=[(sP, kv_p), (sS, kv_sS)], inc=sV)
    E.emit("vector", lambda eng: eng.memset(A["ksml"][:], 0.0), inc=sV)
    for h in range(H):
        pb = (h % 2) * 64
        c = h // 2
        E.emit("vector", (lambda h=h, pb=pb, c=c: lambda eng: eng.tensor_copy(
            A["ksml"][pb:pb + 64, c * 16 + h: c * 16 + h + 1],
            P[7][pb:pb + 64, c:c + 1]))(), inc=sV)
    kv_ready = E.sem_val(sV)
    tr_free[6] = (sV, kv_ready)
    tr_free[7] = (sV, kv_ready)
    rot_free[4] = tr_free[4]
    rot_free[5] = tr_free[5]

    def dbg_store(t, tile, v):
        E.emit("scalar", (lambda t=t, tile=tile: lambda eng: eng.dma_start(
            out=out_tiled[t], in_=tile.rearrange("p (s d) -> p s d", d=D)))(),
               waits=[(sV, v)], inc=qX, amt=16)
        state["store_done"] = (qX, E.sem_val(qX))
        state["xbuf_free"] = (sV, v)
        state["sel_gate"] = state["store_done"]

    def pass2_tile(t):
        import os
        if os.environ.get("SERIAL_LOADS"):
            ring["extra"] = state["store_done"]
        waits = []
        if state["xbuf_free"] is not None:
            waits.append(state["xbuf_free"])
        if state["store_done"] is not None:
            waits.append(state["store_done"])
        selw = [] if state.get("sel_gate") is None else [state["sel_gate"]]
        E.emit("sync", lambda eng: eng.dma_start(out=A["selA"][:], in_=sel_h),
               waits=selw, inc=qSel, amt=16)
        sel_ld = E.sem_val(qSel)
        if state.get("early") is not None:
            stats_v, sd = state.pop("early")
            u_v = ln_apply(A["xbuf"], A["res1"], stats_v, base=16,
                           extra_waits=[sd, state["store_done"]])
        else:
            E.emit("scalar", (lambda t=t: lambda eng: eng.dma_start(
                out=xbuf_3d[:], in_=x_tiled[t]))(), waits=waits, inc=qX, amt=16)
            x_ld = E.sem_val(qX)
            E.barrier("vector", [(qX, x_ld)])
            E.barrier("scalar", [(qX, x_ld)])
            stats_v = ln_stats(A["xbuf"], A["res2"])
            u_v = ln_apply(A["xbuf"], A["res1"], stats_v)
        fmT_v = tm_to_fm(A["res1"], "p2", [(sV, u_v)], A["fmT"])
        if stage == "u":
            return dbg_store(t, A["res1"], u_v)

        flatq = wb["wq_s"].rearrange("a p d -> p a d")
        q0s, q0t = load_chunk(flatq[:, 0:4, :], "wq_s")
        q1s, q1t = load_chunk(flatq[:, 4:8, :], "wq_s")
        q_done_v = 0
        for cq in range(DC):
            bank = cq % 4
            slot, tok = (q0s, q0t) if cq < 4 else (q1s, q1t)
            ccol = (cq % 4) * 1024
            mats = [((lambda slot=slot, ccol=ccol, c=c: A[f"wbuf{slot}"][:, ccol + c * 128: ccol + (c + 1) * 128]),
                     (lambda c=c: A["fmT"][:, c * 512:(c + 1) * 512]))
                    for c in range(DC)]
            g_done = mm_group(bank, mats, waits=[(sV, fmT_v), chunk_wait(tok)])
            E.emit("vector", (lambda bank=bank, cq=cq: lambda eng: eng.tensor_scalar(
                A["scr4k"][:, 512:1024], P[bank][:, 0:512],
                A["bq_c"][:, cq:cq + 1], 0.0, ALU.add, ALU.min))(),
                waits=[(sP, g_done)], inc=sV)
            E.emit("vector", (lambda bank=bank, cq=cq: lambda eng: eng.tensor_scalar(
                A["scr4k"][:, 0:512], P[bank][:, 0:512],
                A["bq_c"][:, cq:cq + 1], 0.0, ALU.add, ALU.max))(), inc=sV)
            v_ab = E.sem_val(sV)
            rot_free[bank] = (sV, v_ab)
            E.emit("scalar", lambda eng: eng.activation(
                A["scr4k"][:, 512:1024], A["scr4k"][:, 512:1024], AF.Exp),
                waits=[(sV, v_ab)], inc=sS, flush=True)
            s_exp = E.sem_val(sS)
            E.emit("vector", (lambda cq=cq: lambda eng: eng.tensor_tensor(
                A["qbf"][:, cq * 512:(cq + 1) * 512], A["scr4k"][:, 0:512],
                A["scr4k"][:, 512:1024], ALU.add))(), waits=[(sS, s_exp)], inc=sV)
            q_done_v = E.sem_val(sV)
        chunk_consumed(q0t, q0s, sV, q_done_v)
        chunk_consumed(q1t, q1s, sV, q_done_v)

        zw = [(sV, q_done_v)]
        if tr_free[6] is not None:
            zw.append(tr_free[6])
        z_p = 0
        for c in range(DC):
            z_p = E.emit("tensor", (lambda c=c: lambda eng: eng.matmul(
                P[6][0:16, 0:512], A["ksml"][:, c * 16:(c + 1) * 16],
                A["qbf"][:, c * 512:(c + 1) * 512],
                start=(c == 0), stop=(c == DC - 1)))(),
                waits=zw if c == 0 else (), inc=sP if c == DC - 1 else None)
        E.emit("vector", lambda eng: eng.tensor_scalar(
            A["zsb"][:], P[6][0:16, 0:512], EPS_Z, 0.0, ALU.add, ALU.add),
            waits=[(sP, z_p)], inc=sV)
        E.emit("vector", lambda eng: eng.reciprocal(A["zsb"][:], A["zsb"][:]), inc=sV)
        zinv_v = E.sem_val(sV)
        tr_free[6] = (sV, zinv_v)

        o_done_v = 0
        for c in range(DC):
            bank = c % 4
            w = [(sV, max(q_done_v, kv_ready))]
            if rot_free[bank] is not None:
                w.append(rot_free[bank])
            E.emit("tensor", (lambda c=c, bank=bank: lambda eng: eng.matmul(
                P[bank][0:64, 0:512], A["kvsb"][0:64, c * 64:(c + 1) * 64],
                A["qbf"][0:64, c * 512:(c + 1) * 512], start=True, stop=True))(),
                waits=w)
            E.emit("tensor", (lambda c=c, bank=bank: lambda eng: eng.matmul(
                P[bank][64:128, 0:512], A["kvsb"][64:128, c * 64:(c + 1) * 64],
                A["qbf"][64:128, c * 512:(c + 1) * 512], start=True, stop=True))(),
                inc=sP)
            zbank = 6 + (c % 2)
            zslot = (c % 2) * 512
            zbw = [(sV, zinv_v), (qSel, sel_ld)]
            if tr_free[zbank] is not None:
                zbw.append(tr_free[zbank])
            zb_p = E.emit("tensor", (lambda c=c, zbank=zbank: lambda eng: eng.matmul(
                P[zbank][:, 0:512], A["selA"][:, c * 128:(c + 1) * 128], A["zsb"][:],
                start=True, stop=True))(), waits=zbw, inc=sP)
            ztile = "zbsb2" if c % 2 == 0 else "zbsbB"
            E.emit("vector", (lambda zbank=zbank, ztile=ztile: lambda eng: eng.tensor_copy(
                A[ztile][:, 0:512], P[zbank][:, 0:512]))(),
                waits=[(sP, zb_p)], inc=sV)
            zb_v = E.sem_val(sV)
            tr_free[zbank] = (sV, zb_v)
            E.emit("vector", (lambda c=c, bank=bank, ztile=ztile: lambda eng: eng.tensor_tensor(
                A["obf"][:, c * 512:(c + 1) * 512], P[bank][:, 0:512],
                A[ztile][:, 0:512], ALU.mult))(), inc=sV)
            o_done_v = E.sem_val(sV)
            rot_free[bank] = (sV, o_done_v)

        o0s, o0t = load_chunk(wb["wo_t"][:, 0:4, :], "wo_t")
        o1s, o1t = load_chunk(wb["wo_t"][:, 4:8, :], "wo_t")
        a_v = 0
        bank_i = 0
        for s in range(NS):
            for g in range(2):
                bank = bank_i % 4
                bank_i += 1
                mats = []
                for c in range(DC):
                    slot = o0s if c < 4 else o1s
                    cc = c % 4
                    mats.append((
                        (lambda c=c, s=s: A["obf"][:, c * 512 + s * 128: c * 512 + s * 128 + 128]),
                        (lambda slot=slot, cc=cc, g=g: A[f"wbuf{slot}"][:, cc * 1024 + g * 512: cc * 1024 + g * 512 + 512]),
                    ))
                g_done = mm_group(bank, mats,
                                  waits=[(sV, o_done_v), chunk_wait(o0t), chunk_wait(o1t)])
                off = s * 1024 + g * 512
                E.emit("vector", (lambda bank=bank, off=off: lambda eng: eng.tensor_tensor(
                    A["res1"][:, off:off + 512], A["res1"][:, off:off + 512],
                    P[bank][:, 0:512], ALU.add))(), waits=[(sP, g_done)], inc=sV)
                a_v = E.sem_val(sV)
                rot_free[bank] = (sV, a_v)
        chunk_consumed(o0t, o0s, sV, a_v)
        chunk_consumed(o1t, o1s, sV, a_v)
        for s in range(NS):
            E.emit("vector", (lambda s=s: lambda eng: eng.tensor_tensor(
                A["res1"][:, s * 1024:(s + 1) * 1024],
                A["res1"][:, s * 1024:(s + 1) * 1024], A["boB"][:], ALU.add))(), inc=sV)
        a_v = E.sem_val(sV)
        if stage == "a":
            return dbg_store(t, A["res1"], a_v)

        E.barrier("scalar", [(sV, a_v)])
        stats1 = ln_stats(A["res1"], A["res2"])
        n1_v = ln_apply(A["res1"], A["res2"], stats1)
        for s in range(NS):
            E.emit("vector", (lambda s=s: lambda eng: eng.tensor_tensor(
                A["res2"][:, s * 1024:(s + 1) * 1024],
                A["res2"][:, s * 1024:(s + 1) * 1024], A["g1B"][:], ALU.mult))(), inc=sV)
            E.emit("vector", (lambda s=s: lambda eng: eng.tensor_tensor(
                A["res2"][:, s * 1024:(s + 1) * 1024],
                A["res2"][:, s * 1024:(s + 1) * 1024], A["be1B"][:], ALU.add))(), inc=sV)
        x1_v = E.sem_val(sV)
        if stage == "x1":
            return dbg_store(t, A["res2"], x1_v)
        fmT1_v = tm_to_fm(A["res2"], "p2", [(sV, x1_v)], A["fmT"])

        flat1 = wb["w1_s"].rearrange("a p d -> p a d")
        g_v = 0
        w1tok = [None, None]
        for cq in range(FC):
            if cq % 4 == 0:
                i = cq // 4
                w1tok = list(load_chunk(flat1[:, i * 4:(i + 1) * 4, :], "w1_s"))
            bank = cq % 4
            slot, tok = w1tok
            ccol = (cq % 4) * 1024
            mats = [((lambda slot=slot, ccol=ccol, c=c: A[f"wbuf{slot}"][:, ccol + c * 128: ccol + (c + 1) * 128]),
                     (lambda c=c: A["fmT"][:, c * 512:(c + 1) * 512]))
                    for c in range(DC)]
            g_done = mm_group(bank, mats, waits=[(sV, fmT1_v), chunk_wait(tok)])
            E.emit("scalar", (lambda bank=bank, cq=cq: lambda eng: eng.activation(
                A["bigA"][:, cq * 512:(cq + 1) * 512], P[bank][:, 0:512], AF.Gelu,
                bias=A["b1_c"][:, cq:cq + 1], scale=1.0))(),
                waits=[(sP, g_done)], inc=sS)
            g_v = E.sem_val(sS)
            rot_free[bank] = (sS, g_v)
            if cq % 4 == 3:
                chunk_consumed(tok, slot, sS, g_v)
        E.emit("scalar", lambda eng: eng.drain(), inc=sS)
        g_v = E.sem_val(sS)

        y_v = 0
        for cq in range(DC):
            slot, tok = load_chunk(wb["w2_s"][cq], "w2_s")
            bank = cq % 4
            mats = [((lambda slot=slot, c=c: A[f"wbuf{slot}"][:, c * 128:(c + 1) * 128]),
                     (lambda c=c: A["bigA"][:, c * 512:(c + 1) * 512]))
                    for c in range(FC)]
            g_done = mm_group(bank, mats, waits=[(sS, g_v), chunk_wait(tok)])
            E.emit("vector", (lambda bank=bank, cq=cq: lambda eng: eng.tensor_scalar(
                A["scrF"][:, cq * 512:(cq + 1) * 512], P[bank][:, 0:512],
                A["b2_c"][:, cq:cq + 1], 0.0, ALU.add, ALU.add))(),
                waits=[(sP, g_done)], inc=sV)
            y_v = E.sem_val(sV)
            rot_free[bank] = (sV, y_v)
            chunk_consumed(tok, slot, sV, y_v)

        if stage == "yfm":
            return dbg_store(t, A["scrF"], y_v)
        r1_v = fm_to_tm(A["scrF"], "p2", [(sV, y_v)], A["res2"], add=True)
        if stage == "r1":
            return dbg_store(t, A["res2"], r1_v)

        E.barrier("scalar", [(sV, r1_v)])
        stats2 = ln_stats(A["res2"], A["res1"])
        n2_v = ln_apply(A["res2"], A["res2"], stats2)
        for s in range(NS):
            E.emit("vector", (lambda s=s: lambda eng: eng.tensor_tensor(
                A["res2"][:, s * 1024:(s + 1) * 1024],
                A["res2"][:, s * 1024:(s + 1) * 1024], A["g2B"][:], ALU.mult))(), inc=sV)
            E.emit("vector", (lambda s=s: lambda eng: eng.tensor_tensor(
                A["res2"][:, s * 1024:(s + 1) * 1024],
                A["res2"][:, s * 1024:(s + 1) * 1024], A["be2B"][:], ALU.add))(), inc=sV)
        blk_v = E.sem_val(sV)
        E.barrier("scalar", [(sV, blk_v)])
        stats3 = ln_stats(A["res2"], A["res1"])
        v_v = ln_apply(A["res2"], A["res2"], stats3)
        for s in range(NS):
            E.emit("vector", (lambda s=s: lambda eng: eng.tensor_tensor(
                A["res1"][:, s * 1024:(s + 1) * 1024],
                A["xbuf"][:, s * 1024:(s + 1) * 1024],
                A["res2"][:, s * 1024:(s + 1) * 1024], ALU.add))(), inc=sV)
        x2_v = E.sem_val(sV)
        if stage == "x2":
            return dbg_store(t, A["res1"], x2_v)

        E.barrier("scalar", [(sV, x2_v)])
        stats4 = ln_stats(A["res1"], A["res2"])
        ln2_v = ln_apply(A["res1"], A["res2"], stats4)
        if stage == "ln2t":
            return dbg_store(t, A["res2"], ln2_v)
        fmT2_v = tm_to_fm(A["res2"], "p2", [(sV, ln2_v)], A["fmT"])

        early_v = None
        ex_ld = None
        if t + 1 < n_tiles:
            E.emit("scalar", (lambda t=t: lambda eng: eng.dma_start(
                out=xbuf_3d[:], in_=x_tiled[t + 1]))(),
                waits=[(sV, x2_v)], inc=qX, amt=16)
            ex_ld = E.sem_val(qX)

        flatw1 = wb["fw1_s"].rearrange("a p d -> p a d")
        flatw2 = wb["fw2_s"].rearrange("a p d -> p a d")
        t1_v = 0
        gtoks = [None] * 4
        for cq in range(FC):
            if cq % 4 == 0:
                i = cq // 4
                gtoks[0], gtoks[1] = load_chunk(flatw1[:, i * 4:(i + 1) * 4, :], "fw1_s")
                gtoks[2], gtoks[3] = load_chunk(flatw2[:, i * 4:(i + 1) * 4, :], "fw2_s")
            bankA = (2 * cq) % 4
            bankB = (2 * cq + 1) % 4
            ccol = (cq % 4) * 1024
            mats1 = [((lambda slot=gtoks[0], ccol=ccol, c=c: A[f"wbuf{slot}"][:, ccol + c * 128: ccol + (c + 1) * 128]),
                      (lambda c=c: A["fmT"][:, c * 512:(c + 1) * 512]))
                     for c in range(DC)]
            mats2 = [((lambda slot=gtoks[2], ccol=ccol, c=c: A[f"wbuf{slot}"][:, ccol + c * 128: ccol + (c + 1) * 128]),
                      (lambda c=c: A["fmT"][:, c * 512:(c + 1) * 512]))
                     for c in range(DC)]
            gd1 = mm_group(bankA, mats1, waits=[(sV, fmT2_v), chunk_wait(gtoks[1])])
            gd2 = mm_group(bankB, mats2, waits=[(sV, fmT2_v), chunk_wait(gtoks[3])])
            E.emit("scalar", (lambda bankA=bankA, cq=cq: lambda eng: eng.activation(
                A["scr4k"][:, 0:512], P[bankA][:, 0:512], AF.Silu,
                bias=0.0, scale=A["s1_c"][:, cq:cq + 1]))(),
                waits=[(sP, gd1), (sV, t1_v)], inc=sS, flush=True)
            sil_v = E.sem_val(sS)
            rot_free[bankA] = (sS, sil_v)
            E.emit("vector", (lambda bankB=bankB, cq=cq: lambda eng: eng.scalar_tensor_tensor(
                A["bigA"][:, cq * 512:(cq + 1) * 512], P[bankB][:, 0:512],
                A["s2_c"][:, cq:cq + 1], A["scr4k"][:, 0:512], ALU.mult, ALU.mult))(),
                waits=[(sP, gd2), (sS, sil_v)], inc=sV)
            t1_v = E.sem_val(sV)
            rot_free[bankB] = (sV, t1_v)
            if cq % 4 == 3:
                chunk_consumed(gtoks[1], gtoks[0], sV, t1_v)
                chunk_consumed(gtoks[3], gtoks[2], sV, t1_v)

        if ex_ld is not None:
            E.barrier("vector", [(qX, ex_ld)])
            E.barrier("scalar", [(qX, ex_ld)])
            early_v = ln_stats(A["xbuf"], A["scrF"], base=16)
            state["early"] = (early_v, (qX, ex_ld))

        def big_ff(wname, src, dst, scol, prev):
            last_s = 0
            for cq in range(FC):
                slot, tok = load_chunk(wb[wname][cq], wname)
                bank = cq % 6
                mats = [((lambda slot=slot, c=c: A[f"wbuf{slot}"][:, c * 128:(c + 1) * 128]),
                         (lambda c=c, src=src: src[:, c * 512:(c + 1) * 512]))
                        for c in range(FC)]
                g_done = mm_group(bank, mats, waits=[prev, chunk_wait(tok)])
                E.emit("scalar", (lambda bank=bank, cq=cq, dst=dst, scol=scol:
                                  lambda eng: eng.activation(
                    dst[:, cq * 512:(cq + 1) * 512], P[bank][:, 0:512], AF.Gelu,
                    bias=0.0, scale=A[scol][:, cq:cq + 1]))(),
                    waits=[(sP, g_done)], inc=sS)
                last_s = E.sem_val(sS)
                rot_free[bank] = (sS, last_s)
                chunk_consumed(tok, slot, sS, last_s)
            E.emit("scalar", lambda eng: eng.drain(), inc=sS)
            return E.sem_val(sS)

        t2_v = big_ff("h1_s", A["bigA"], A["bigB"], "sh1_c", (sV, t1_v))
        t3_v = big_ff("h2_s", A["bigB"], A["bigA"], "sh2_c", (sS, t2_v))

        f_v = 0
        for cq in range(DC):
            slot, tok = load_chunk(wb["w3_s"][cq], "w3_s")
            bank = cq % 4
            mats = [((lambda slot=slot, c=c: A[f"wbuf{slot}"][:, c * 128:(c + 1) * 128]),
                     (lambda c=c: A["bigA"][:, c * 512:(c + 1) * 512]))
                    for c in range(FC)]
            g_done = mm_group(bank, mats, waits=[(sS, t3_v), chunk_wait(tok)])
            wv = [(sP, g_done)]
            if cq == 0 and early_v is not None:
                wv.append((sV, early_v))
            E.emit("vector", (lambda bank=bank, cq=cq: lambda eng: eng.tensor_scalar(
                A["scrF"][:, cq * 512:(cq + 1) * 512], P[bank][:, 0:512],
                A["s3_c"][:, cq:cq + 1], 0.0, ALU.mult, ALU.add))(),
                waits=wv, inc=sV)
            f_v = E.sem_val(sV)
            rot_free[bank] = (sV, f_v)
            chunk_consumed(tok, slot, sV, f_v)
        state["sel_gate"] = (sV, f_v)

        if stage == "ffm":
            return dbg_store(t, A["scrF"], f_v)
        ftm_v = fm_to_tm(A["scrF"], "p2", [(sV, f_v)], A["res2"], add=False)
        E.barrier("scalar", [(sV, ftm_v)])
        stats5 = ln_stats(A["res2"], A["scrF"])
        nf_v = ln_apply(A["res2"], A["res2"], stats5)
        for s in range(NS):
            E.emit("vector", (lambda s=s: lambda eng: eng.tensor_tensor(
                A["res1"][:, s * 1024:(s + 1) * 1024],
                A["res1"][:, s * 1024:(s + 1) * 1024],
                A["res2"][:, s * 1024:(s + 1) * 1024], ALU.add))(), inc=sV)
        out_v = E.sem_val(sV)

        E.emit("scalar", (lambda t=t: lambda eng: eng.dma_start(
            out=out_tiled[t], in_=A["res1"].rearrange("p (s d) -> p s d", d=D)))(),
            waits=[(sV, out_v)], inc=qX, amt=16)
        state["store_done"] = (qX, E.sem_val(qX))
        state["xbuf_free"] = (sV, out_v)

    for t in range(n_tiles):
        pass2_tile(t)

    E.barrier("scalar", [(qX, E.sem_val(qX))])
    return finish(nc, E, stack)


def finish(nc, E, stack):
    with nc.Block() as blk:
        def make_body(eng_name):
            def body(eng):
                for fn, waits, inc in E.ev[eng_name]:
                    for sem, val in waits:
                        eng.wait_ge(sem, val)
                    if fn is None:
                        assert inc is None
                        continue
                    inst = fn(eng)
                    if inc is not None:
                        inst.then_inc(inc[0], inc[1])
            return body
        blk.sync(make_body("sync"))
        blk.gpsimd(make_body("gpsimd"))
        blk.scalar(make_body("scalar"))
        blk.vector(make_body("vector"))
        blk.tensor(make_body("tensor"))
    stack.close()
    return nc


def _prep_inputs(ins):
    f32 = np.float32

    def fm_swizzle(w_in_out, co_n, ci_n):
        a = np.asarray(w_in_out, f32).reshape(ci_n, 128, co_n, 128)
        return np.ascontiguousarray(a.transpose(2, 1, 0, 3).reshape(co_n, 128, ci_n * 128))

    def tm_swizzle(w_in_out):
        a = np.asarray(w_in_out, f32).reshape(8, 128, -1)
        return np.ascontiguousarray(a.transpose(1, 0, 2))

    def colmat(v, n):
        return np.ascontiguousarray(np.asarray(v, f32).reshape(n, 128).T)

    d = {}
    d["wk_t"] = tm_swizzle(ins["L0_attn_Wk"])
    d["wv_t"] = tm_swizzle(ins["L0_attn_Wv"])
    d["wo_t"] = tm_swizzle(ins["L0_attn_Wo"])
    d["wq_s"] = fm_swizzle(ins["L0_attn_Wq"], 8, 8)
    d["w1_s"] = fm_swizzle(ins["L0_attn_W1"], 32, 8)
    d["w2_s"] = fm_swizzle(ins["L0_attn_W2"], 8, 32)
    d["fw1_s"] = fm_swizzle(np.asarray(ins["L0_ffn_w1_w"], f32).T, 32, 8)
    d["fw2_s"] = fm_swizzle(np.asarray(ins["L0_ffn_w2_w"], f32).T, 32, 8)
    d["h1_s"] = fm_swizzle(np.asarray(ins["L0_ffn_h1_w"], f32).T, 32, 32)
    d["h2_s"] = fm_swizzle(np.asarray(ins["L0_ffn_h2_w"], f32).T, 32, 32)
    d["w3_s"] = fm_swizzle(np.asarray(ins["L0_ffn_w3_w"], f32).T, 8, 32)
    for k, src in [("bk", "L0_attn_bk"), ("bv", "L0_attn_bv"), ("bo", "L0_attn_bo"),
                   ("g1", "L0_attn_g1"), ("be1", "L0_attn_be1"),
                   ("g2", "L0_attn_g2"), ("be2", "L0_attn_be2")]:
        d[k] = np.ascontiguousarray(np.asarray(ins[src], f32))
    d["bq_c"] = colmat(ins["L0_attn_bq"], 8)
    d["b1_c"] = colmat(ins["L0_attn_b1"], 32)
    d["b2_c"] = colmat(ins["L0_attn_b2"], 8)
    d["s1_c"] = colmat(ins["L0_ffn_w1_s"], 32)
    d["s2_c"] = colmat(ins["L0_ffn_w2_s"], 32)
    d["sh1_c"] = colmat(ins["L0_ffn_h1_s"], 32)
    d["sh2_c"] = colmat(ins["L0_ffn_h2_s"], 32)
    d["s3_c"] = colmat(ins["L0_ffn_w3_s"], 8)
    d["id_f32"] = np.eye(128, dtype=f32)
    sel = np.zeros((16, 1024), f32)
    for c in range(8):
        sel[2 * c, c * 128:c * 128 + 64] = 1.0
        sel[2 * c + 1, c * 128 + 64:c * 128 + 128] = 1.0
    d["sel_f32"] = sel
    return d


def kernel(**inputs):
    shared = _prep_inputs(inputs)
    x = np.asarray(inputs["L0_attn_src"], np.float32)
    nc = build("full")
    in_maps = []
    for b in range(8):
        m = dict(shared)
        m["x"] = np.ascontiguousarray(x[b])
        in_maps.append(m)
    res = run_bass_kernel_spmd(nc, in_maps, core_ids=list(range(8)))
    out = np.stack([np.asarray(r["out"]) for r in res.results], axis=0)
    return out.astype(np.float32)


if __name__ == "__main__":
    print("building program...")
    nc = build("full")
    print("n_instructions:", nc.n_instructions())


# version 10
# speedup vs baseline: 1.0766x; 1.0432x over previous
import contextlib
import numpy as np
import concourse.bass as bass
import concourse.mybir as mybir
from concourse.bass_utils import run_bass_kernel_spmd

F32 = mybir.dt.float32
BF16 = mybir.dt.bfloat16
AF = mybir.ActivationFunctionType
ALU = mybir.AluOpType
AX = mybir.AxisListType

S, D, H, DH, FF = 4096, 1024, 16, 64, 4096
T = 512
NT = S // T
NS = T // 128
DC = D // 128
FC = FF // 128
EPS_LN = 1e-5
EPS_Z = 1e-6
KB = 1024

ARENA = 16512
OFF = dict(xbuf=ARENA, res1=ARENA + 16 * KB, res2=ARENA + 32 * KB,
           fmT=ARENA + 48 * KB, scrF=ARENA + 56 * KB, bigA=ARENA + 72 * KB,
           bigB=ARENA + 104 * KB, wbuf=ARENA + 136 * KB, consts=ARENA + 168 * KB)


def _roundup(x, a):
    return (x + a - 1) // a * a


class Emitter:
    def __init__(self):
        self.ev = {k: [] for k in ("sync", "gpsimd", "scalar", "vector", "tensor")}
        self.ctr = {}
        self.sems = {}
        self.bounce_cache = {}

    def sem_val(self, sem):
        return self.ctr.get(sem.name, 0)

    def bounce(self, sem, val):
        key = (sem.name, val)
        if key in self.bounce_cache:
            return self.bounce_cache[key]
        via = "scalar" if sem.name == "sV" else "vector"
        via_sem = self.sems["sS"] if via == "scalar" else self.sems["sV"]
        self.ev[via].append((None, [(sem, val)], None))
        self.ctr[via_sem.name] = self.ctr.get(via_sem.name, 0) + 1
        self.ev[via].append((lambda eng_: eng_.drain(), [], (via_sem, 1)))
        out = (via_sem, self.ctr[via_sem.name])
        self.bounce_cache[key] = out
        return out

    def emit(self, eng, fn, waits=(), inc=None, amt=1, flush=False):
        w = []
        for (sem, val) in waits:
            if val is None or val <= 0:
                continue
            assert val <= self.ctr.get(sem.name, 0), \
                f"wait {sem.name}>={val} but only {self.ctr.get(sem.name, 0)} emitted"
            w.append((sem, val))
        if inc is not None and amt == 1 and eng in ("vector", "scalar", "gpsimd"):
            self.ev[eng].append((fn, w, None))
            self.ctr[inc.name] = self.ctr.get(inc.name, 0) + amt
            self.ev[eng].append((lambda eng_: eng_.drain(), [], (inc, amt)))
            return self.ctr.get(inc.name, 0)
        incv = None
        if inc is not None:
            self.ctr[inc.name] = self.ctr.get(inc.name, 0) + amt
            incv = (inc, amt)
        self.ev[eng].append((fn, w, incv))
        return self.ctr.get(inc.name, 0) if inc is not None else None

    def barrier(self, eng, waits):
        self.emit(eng, None, waits=waits)


def build(stage="full", n_tiles=NT):
    nc = bass.Bass("TRN2", target_bir_lowering=False, debug=False)

    x_h = nc.dram_tensor("x", [S, D], F32, kind="ExternalInput").ap()
    out_h = nc.dram_tensor("out", [S, D], F32, kind="ExternalOutput").ap()

    wf = {}
    for name, shape in [
        ("wk_t", [128, DC, D]), ("wv_t", [128, DC, D]), ("wo_t", [128, DC, D]),
        ("wq_s", [DC, 128, D]),
        ("w1_s", [FC, 128, D]), ("w2_s", [DC, 128, FF]),
        ("fw1_s", [FC, 128, D]), ("fw2_s", [FC, 128, D]),
        ("h1_s", [FC, 128, FF]), ("h2_s", [FC, 128, FF]),
        ("w3_s", [DC, 128, FF]),
    ]:
        wf[name] = nc.dram_tensor(name, shape, F32, kind="ExternalInput").ap()
    wb = {name: nc.dram_tensor(name + "_b", list(ap.shape), BF16, kind="Internal").ap()
          for name, ap in wf.items()}

    vecs = {name: nc.dram_tensor(name, [D], F32, kind="ExternalInput").ap()
            for name in ("bk", "bv", "bo", "g1", "be1", "g2", "be2")}
    cols_h = {}
    for name, w in [("bq_c", DC), ("b2_c", DC), ("s3_c", DC), ("b1_c", FC),
                    ("s1_c", FC), ("s2_c", FC), ("sh1_c", FC), ("sh2_c", FC)]:
        cols_h[name] = nc.dram_tensor(name, [128, w], F32, kind="ExternalInput").ap()
    id_h = nc.dram_tensor("id_f32", [128, 128], F32, kind="ExternalInput").ap()
    sel_h = nc.dram_tensor("sel_f32", [16, 1024], F32, kind="ExternalInput").ap()

    A = {}

    def sba(name, shape, dtype, off):
        A[name] = nc.alloc_sbuf_tensor_at(name, shape, dtype, offset=off).ap()

    sba("xbuf", [128, 4096], F32, OFF["xbuf"])
    sba("res1", [128, 4096], F32, OFF["res1"])
    sba("res2", [128, 4096], F32, OFF["res2"])
    sba("fmT", [128, 4096], BF16, OFF["fmT"])
    sba("kbf", [128, 4096], BF16, OFF["scrF"])
    sba("vbf", [128, 4096], BF16, OFF["scrF"] + 8 * KB)
    sba("qbf", [128, 4096], BF16, OFF["scrF"])
    sba("obf", [128, 4096], BF16, OFF["scrF"] + 8 * KB)
    sba("scrF", [128, 4096], F32, OFF["scrF"])
    sba("bigA", [128, 16384], BF16, OFF["bigA"])
    sba("bigB", [128, 16384], BF16, OFF["bigB"])
    sba("selA", [16, 1024], F32, OFF["bigA"] + 28 * KB)
    sba("zbsbB", [128, 512], F32, OFF["bigB"] + 30 * KB)
    sba("outstg", [128, 4096], F32, OFF["bigB"])
    for i in range(4):
        sba(f"wbuf{i}", [128, 4096], BF16, OFF["wbuf"] + i * 8 * KB)

    co = OFF["consts"]

    def calloc(name, shape, dtype):
        nonlocal co
        n = int(np.prod(shape[1:]))
        sba(name, shape, dtype, co)
        co += _roundup(n * (2 if dtype == BF16 else 4), 32)

    for nm in ("bkB", "bvB", "boB", "g1B", "be1B", "g2B", "be2B"):
        calloc(nm, [128, 1024], F32)
    for nm, w in [("bq_c", DC), ("b2_c", DC), ("s3_c", DC), ("b1_c", FC),
                  ("s1_c", FC), ("s2_c", FC), ("sh1_c", FC), ("sh2_c", FC)]:
        calloc(nm, [128, w], F32)
    calloc("idf", [128, 128], F32)
    calloc("kvsb", [128, 512], BF16)
    calloc("ksml", [128, 128], BF16)
    calloc("zsb", [16, 512], F32)
    calloc("zbsb2", [128, 512], F32)
    calloc("ones", [128, 1], BF16)
    calloc("stats", [128, 16], F32)
    calloc("pad8", [128, 8], F32)
    calloc("scr4k", [128, 1024], F32)
    assert co <= 229344, f"consts overflow: {co}"

    P = [nc.alloc_psum_tensor(f"p{i}", [128, 512], F32).ap() for i in range(8)]

    E = Emitter()
    stack = contextlib.ExitStack()
    qW = [stack.enter_context(nc.semaphore(f"qW{i}")) for i in range(4)]
    qK = stack.enter_context(nc.semaphore("qK"))
    qSel = stack.enter_context(nc.semaphore("qSel"))
    qX = stack.enter_context(nc.semaphore("qX"))
    qC = stack.enter_context(nc.semaphore("qC"))
    sP = stack.enter_context(nc.semaphore("sP"))
    sV = stack.enter_context(nc.semaphore("sV"))
    sS = stack.enter_context(nc.semaphore("sS"))
    E.sems = {"sV": sV, "sS": sS}

    for nm, dst in [("bk", "bkB"), ("bv", "bvB"), ("bo", "boB"), ("g1", "g1B"),
                    ("be1", "be1B"), ("g2", "g2B"), ("be2", "be2B")]:
        E.emit("sync", (lambda d=A[dst], s=vecs[nm]: lambda eng: eng.dma_start(
            out=d[:], in_=s.unsqueeze(0).partition_broadcast(128)))(), inc=qK, amt=16)
    for nm in ("bq_c", "b2_c", "s3_c", "b1_c", "s1_c", "s2_c", "sh1_c", "sh2_c"):
        E.emit("sync", (lambda d=A[nm], s=cols_h[nm]: lambda eng: eng.dma_start(
            out=d[:], in_=s))(), inc=qK, amt=16)
    E.emit("sync", lambda eng: eng.dma_start(out=A["idf"][:], in_=id_h), inc=qK, amt=16)
    consts_loaded = E.sem_val(qK)

    cast_done = {}
    for name in ("wk_t", "wv_t", "wq_s", "wo_t", "w1_s", "w2_s",
                 "fw1_s", "fw2_s", "h1_s", "h2_s", "w3_s"):
        src, dst = wf[name], wb[name]
        n0 = src.shape[0]
        pieces = 4 if int(np.prod(src.shape)) * 4 > 20e6 else 1
        step = n0 // pieces
        for i in range(pieces):
            prev = E.sem_val(qC)
            E.emit("gpsimd", (lambda d=dst, s=src, a=i * step, b=(i + 1) * step:
                              lambda eng: eng.dma_start(out=d[a:b], in_=s[a:b]))(),
                   waits=[(qC, prev)], inc=qC, amt=16)
        cast_done[name] = E.sem_val(qC)

    ring = {"idx": 0, "loadedv": {}, "freed": [None] * 4}

    def load_chunk(dram_ap, matrix):
        slot = ring["idx"] % 4
        waits = [(qC, cast_done[matrix])]
        if ring["freed"][slot] is not None:
            waits.append(ring["freed"][slot])
        if ring.get("extra"):
            waits.append(ring["extra"])
        if len(dram_ap.shape) == 3:
            dview = A[f"wbuf{slot}"].rearrange("p (a d) -> p a d", d=dram_ap.shape[2])
        else:
            dview = A[f"wbuf{slot}"][:]
        E.emit("sync", (lambda d=dview, s=dram_ap:
                        lambda eng: eng.dma_start(out=d, in_=s))(),
               waits=waits, inc=qW[slot], amt=16)
        tok = ring["idx"]
        ring["loadedv"][tok] = (qW[slot], E.sem_val(qW[slot]))
        ring["idx"] += 1
        return slot, tok

    def chunk_wait(tok):
        return ring["loadedv"][tok]

    def chunk_consumed(tok, slot, sem, val):
        ring["freed"][slot] = (sem, val)

    def ln_stats(src, scratch, src_ready=None):
        st = A["stats"]
        for s in range(NS):
            ws = list(src_ready or []) if s == 0 else []
            E.emit("scalar", (lambda s=s, scr=scratch: lambda eng: eng.activation(
                scr[:, s * 1024:(s + 1) * 1024], src[:, s * 1024:(s + 1) * 1024],
                AF.Square))(), waits=ws,
                inc=sS if s == NS - 1 else None, flush=(s == NS - 1))
        sq_done = E.sem_val(sS)
        for s in range(NS):
            E.emit("vector", (lambda s=s: lambda eng: eng.reduce_sum(
                st[:, s:s + 1], src[:, s * 1024:(s + 1) * 1024], axis=AX.X))(),
                inc=sV)
        for s in range(NS):
            E.emit("vector", (lambda s=s, scr=scratch: lambda eng: eng.reduce_sum(
                st[:, 4 + s:5 + s], scr[:, s * 1024:(s + 1) * 1024], axis=AX.X))(),
                waits=[(sS, sq_done)] if s == 0 else (), inc=sV)
        for j in range(3):
            E.emit("vector", (lambda j=j: lambda eng: eng.memset(A["pad8"][:], float(j)))())
        E.emit("vector", lambda eng: eng.tensor_scalar(
            st[:, 0:4], st[:, 0:4], 1.0 / D, 0.0, ALU.mult, ALU.add), inc=sV)
        E.emit("vector", lambda eng: eng.tensor_scalar(
            st[:, 4:8], st[:, 4:8], 1.0 / D, EPS_LN, ALU.mult, ALU.add), inc=sV)
        E.emit("vector", lambda eng: eng.tensor_tensor(
            st[:, 8:12], st[:, 0:4], st[:, 0:4], ALU.mult), inc=sV)
        E.emit("vector", lambda eng: eng.tensor_tensor(
            st[:, 4:8], st[:, 4:8], st[:, 8:12], ALU.subtract), inc=sV)
        v_var = E.sem_val(sV)
        E.emit("scalar", lambda eng: eng.activation(
            st[:, 4:8], st[:, 4:8], AF.Sqrt),
            waits=[(sV, v_var)], inc=sS, flush=True)
        s_sqrt = E.sem_val(sS)
        E.emit("vector", lambda eng: eng.reciprocal(st[:, 4:8], st[:, 4:8]),
               waits=[(sS, s_sqrt)], inc=sV)
        return E.sem_val(sV)

    def ln_apply(src, dst, stats_ready):
        st = A["stats"]
        for s in range(NS):
            E.emit("vector", (lambda s=s: lambda eng: eng.tensor_scalar(
                dst[:, s * 1024:(s + 1) * 1024], src[:, s * 1024:(s + 1) * 1024],
                st[:, s:s + 1], st[:, 4 + s:5 + s], ALU.subtract, ALU.mult))(),
                waits=[(sV, stats_ready)] if s == 0 else (), inc=sV)
        return E.sem_val(sV)

    tr_banks = {"p1": (4, 5), "p2": (6, 7)}
    tr_free = {4: None, 5: None, 6: None, 7: None}
    rot_free = {i: None for i in range(6)}

    def transpose_sweep(src, which, ready_w, evac):
        b0, b1 = tr_banks[which]
        last_v = 0
        for s in range(NS):
            for half in range(2):
                bank = b0 if half == 0 else b1
                waits = list(ready_w)
                if tr_free[bank] is not None:
                    waits.append(tr_free[bank])
                for j in range(4):
                    c = half * 4 + j
                    E.emit("tensor", (lambda s=s, c=c, j=j, bank=bank, src=src:
                                      lambda eng: eng.transpose(
                        P[bank][:, j * 128:(j + 1) * 128],
                        src[:, ln_tr_in(s, c)], A["idf"][:]))(),
                        waits=waits if j == 0 else (),
                        inc=sP if j == 3 else None)
                p_done = E.sem_val(sP)
                last_v = evac(s, half, bank, p_done)
                tr_free[bank] = (sV, last_v)
        return last_v

    def ln_tr_in(s, c):
        return slice(s * 1024 + c * 128, s * 1024 + (c + 1) * 128)

    def tm_to_fm(src, which, ready_w, dst):
        def evac(s, half, bank, p_done):
            E.emit("vector", (lambda s=s, half=half, bank=bank, d=dst:
                              lambda eng: eng.tensor_copy(
                d.rearrange("p (c x) -> p c x", x=512)[:, half * 4:half * 4 + 4,
                                                       s * 128:s * 128 + 128],
                P[bank].rearrange("p (j x) -> p j x", x=128)))(),
                waits=[(sP, p_done)], inc=sV)
            return E.sem_val(sV)
        return transpose_sweep(src, which, ready_w, evac)

    def fm_to_tm(src_fm, which, ready_w, dst_tm, add):
        b0, b1 = tr_banks[which]
        last_v = 0
        for s in range(NS):
            for half in range(2):
                bank = b0 if half == 0 else b1
                waits = list(ready_w)
                if tr_free[bank] is not None:
                    waits.append(tr_free[bank])
                for j in range(4):
                    c = half * 4 + j
                    E.emit("tensor", (lambda s=s, c=c, j=j, bank=bank, src=src_fm:
                                      lambda eng: eng.transpose(
                        P[bank][:, j * 128:(j + 1) * 128],
                        src[:, c * 512 + s * 128: c * 512 + s * 128 + 128],
                        A["idf"][:]))(),
                        waits=waits if j == 0 else (),
                        inc=sP if j == 3 else None)
                p_done = E.sem_val(sP)
                dview = dst_tm.rearrange("p (s c x) -> p s c x", s=NS, x=128)
                if add:
                    E.emit("vector", (lambda s=s, half=half, bank=bank, dv=dview:
                                      lambda eng: eng.tensor_tensor(
                        dv[:, s, half * 4:half * 4 + 4, :],
                        dv[:, s, half * 4:half * 4 + 4, :],
                        P[bank].rearrange("p (j x) -> p j x", x=128), ALU.add))(),
                        waits=[(sP, p_done)], inc=sV)
                else:
                    E.emit("vector", (lambda s=s, half=half, bank=bank, dv=dview:
                                      lambda eng: eng.tensor_copy(
                        dv[:, s, half * 4:half * 4 + 4, :],
                        P[bank].rearrange("p (j x) -> p j x", x=128)))(),
                        waits=[(sP, p_done)], inc=sV)
                last_v = E.sem_val(sV)
                tr_free[bank] = (sV, last_v)
        return last_v

    def mm_group(bank, matmuls, waits=()):
        n = len(matmuls)
        for i, fn in enumerate(matmuls):
            w = []
            if i == 0:
                if rot_free[bank] is not None:
                    w.append(rot_free[bank])
                w += list(waits)
            E.emit("tensor", (lambda fn=fn, i=i, bank=bank, n=n:
                              lambda eng: eng.matmul(
                P[bank][:, 0:512], fn[0](), fn[1](),
                start=(i == 0), stop=(i == n - 1)))(),
                waits=w, inc=sP if i == n - 1 else None)
        return E.sem_val(sP)

    x_tiled = x_h.rearrange("(t s p) d -> t p s d", p=128, s=NS)
    out_tiled = out_h.rearrange("(t s p) d -> t p s d", p=128, s=NS)
    xbuf_3d = A["xbuf"].rearrange("p (s d) -> p s d", d=D)

    state = {"xbuf_free": None, "store_done": None, "kv_first": True,
             "ks_first": True}

    def pass1_tile(t):
        waits = []
        if state["xbuf_free"] is not None:
            waits.append(state["xbuf_free"])
        E.emit("scalar", (lambda t=t: lambda eng: eng.dma_start(
            out=xbuf_3d[:], in_=x_tiled[t]))(), waits=waits, inc=qX, amt=16)
        x_ld = E.sem_val(qX)
        E.barrier("vector", [(qX, x_ld), (qK, consts_loaded if t == 0 else 0)])
        E.barrier("scalar", [(qX, x_ld)])
        stats_v = ln_stats(A["xbuf"], A["res2"])
        u_v = ln_apply(A["xbuf"], A["res1"], stats_v)
        state["xbuf_free"] = (sV, u_v)
        fmT_v = tm_to_fm(A["res1"], "p1", [(sV, u_v)], A["fmT"])

        def proj_tm(wname, dst_bf, is_k):
            s0, t0c = load_chunk(wb[wname][:, 0:4, :], wname)
            s1, t1c = load_chunk(wb[wname][:, 4:8, :], wname)
            lastV = 0
            bank_i = 0
            for s in range(NS):
                for g in range(2):
                    bank = bank_i % 4
                    bank_i += 1
                    mats = []
                    for c in range(DC):
                        slot = s0 if c < 4 else s1
                        cc = c % 4
                        mats.append((
                            (lambda c=c, s=s: A["fmT"][:, c * 512 + s * 128: c * 512 + s * 128 + 128]),
                            (lambda slot=slot, cc=cc, g=g: A[f"wbuf{slot}"][:, cc * 1024 + g * 512: cc * 1024 + g * 512 + 512]),
                        ))
                    g_done = mm_group(bank, mats,
                                      waits=[(sV, fmT_v), chunk_wait(t0c), chunk_wait(t1c)])
                    off = s * 1024 + g * 512
                    if is_k:
                        E.emit("vector", (lambda bank=bank, g=g: lambda eng: eng.tensor_tensor(
                            A["scr4k"][:, 0:512], P[bank][:, 0:512],
                            A["bkB"][:, g * 512:(g + 1) * 512], ALU.add))(),
                            waits=[(sP, g_done)], inc=sV)
                        v_tt = E.sem_val(sV)
                        rot_free[bank] = (sV, v_tt)
                        E.emit("vector", lambda eng: eng.tensor_scalar(
                            A["scr4k"][:, 512:1024], A["scr4k"][:, 0:512],
                            0.0, 0.0, ALU.min, ALU.add), inc=sV)
                        v_min = E.sem_val(sV)
                        E.emit("scalar", lambda eng: eng.activation(
                            A["scr4k"][:, 512:1024], A["scr4k"][:, 512:1024], AF.Exp),
                            waits=[(sV, v_min)], inc=sS, flush=True)
                        s_exp = E.sem_val(sS)
                        E.emit("vector", (lambda off=off, dst=dst_bf: lambda eng: eng.scalar_tensor_tensor(
                            dst[:, off:off + 512], A["scr4k"][:, 0:512], 0.0,
                            A["scr4k"][:, 512:1024], ALU.max, ALU.add))(),
                            waits=[(sS, s_exp)], inc=sV)
                        lastV = E.sem_val(sV)
                    else:
                        E.emit("vector", (lambda bank=bank, g=g, off=off, dst=dst_bf:
                                          lambda eng: eng.tensor_tensor(
                            dst[:, off:off + 512], P[bank][:, 0:512],
                            A["bvB"][:, g * 512:(g + 1) * 512], ALU.add))(),
                            waits=[(sP, g_done)], inc=sV)
                        lastV = E.sem_val(sV)
                        rot_free[bank] = (sV, lastV)
            chunk_consumed(t0c, s0, sV, lastV)
            chunk_consumed(t1c, s1, sV, lastV)
            return lastV

        k_done = proj_tm("wk_t", A["kbf"], True)
        v_done = proj_tm("wv_t", A["vbf"], False)

        if t == 0:
            E.emit("vector", lambda eng: eng.memset(A["ones"][:], 1.0), inc=sV)
        ones_v = E.sem_val(sV)

        kv_p = 0
        for s in range(NS):
            for h in range(H):
                pb = (h % 2) * 64
                c = h // 2
                w = [(sV, max(k_done, v_done, ones_v))] if (s == 0 and h == 0) else []
                last = (t == n_tiles - 1 and s == NS - 1 and h == H - 1)
                E.emit("tensor", (lambda s=s, h=h, pb=pb, c=c, last=last: lambda eng: eng.matmul(
                    P[6][pb:pb + 64, c * 64:(c + 1) * 64],
                    A["kbf"][:, s * 1024 + h * 64: s * 1024 + (h + 1) * 64],
                    A["vbf"][:, s * 1024 + h * 64: s * 1024 + (h + 1) * 64],
                    start=state["kv_first"], stop=last))(), waits=w)
                state["kv_first"] = False
                kv_p = E.emit("tensor", (lambda s=s, h=h, pb=pb, c=c, last=last: lambda eng: eng.matmul(
                    P[7][pb:pb + 64, c:c + 1],
                    A["kbf"][:, s * 1024 + h * 64: s * 1024 + (h + 1) * 64],
                    A["ones"][:, 0:1],
                    start=state["ks_first"], stop=last))(),
                    inc=sP if last else None)
                state["ks_first"] = False
        return kv_p

    for t in range(n_tiles):
        kv_p = pass1_tile(t)

    E.barrier("scalar", [(sP, kv_p)])
    E.emit("scalar", lambda eng: eng.drain(), inc=sS)
    kv_sS = E.sem_val(sS)
    E.emit("vector", lambda eng: eng.tensor_copy(A["kvsb"][:], P[6][:]),
           waits=[(sP, kv_p), (sS, kv_sS)], inc=sV)
    E.emit("vector", lambda eng: eng.memset(A["ksml"][:], 0.0), inc=sV)
    for h in range(H):
        pb = (h % 2) * 64
        c = h // 2
        E.emit("vector", (lambda h=h, pb=pb, c=c: lambda eng: eng.tensor_copy(
            A["ksml"][pb:pb + 64, c * 16 + h: c * 16 + h + 1],
            P[7][pb:pb + 64, c:c + 1]))(), inc=sV)
    kv_ready = E.sem_val(sV)
    tr_free[6] = (sV, kv_ready)
    tr_free[7] = (sV, kv_ready)
    rot_free[4] = tr_free[4]
    rot_free[5] = tr_free[5]

    def dbg_store(t, tile, v):
        E.emit("scalar", (lambda t=t, tile=tile: lambda eng: eng.dma_start(
            out=out_tiled[t], in_=tile.rearrange("p (s d) -> p s d", d=D)))(),
               waits=[(sV, v)], inc=qX, amt=16)
        state["store_done"] = (qX, E.sem_val(qX))
        state["xbuf_free"] = (sV, v)
        state["sel_gate"] = state["store_done"]

    def pass2_tile(t):
        waits = []
        if state["xbuf_free"] is not None:
            waits.append(state["xbuf_free"])
        E.emit("scalar", (lambda t=t: lambda eng: eng.dma_start(
            out=xbuf_3d[:], in_=x_tiled[t]))(), waits=waits, inc=qX, amt=16)
        x_ld = E.sem_val(qX)
        selw = [] if state.get("sel_gate") is None else [state["sel_gate"]]
        E.emit("sync", lambda eng: eng.dma_start(out=A["selA"][:], in_=sel_h),
               waits=selw, inc=qSel, amt=16)
        sel_ld = E.sem_val(qSel)
        E.barrier("vector", [(qX, x_ld)])
        E.barrier("scalar", [(qX, x_ld)] +
                  ([state["res_free"]] if state.get("res_free") else []))
        stats_v = ln_stats(A["xbuf"], A["res2"])
        u_v = ln_apply(A["xbuf"], A["res1"], stats_v)
        fmT_v = tm_to_fm(A["res1"], "p2", [(sV, u_v)], A["fmT"])
        if stage == "u":
            return dbg_store(t, A["res1"], u_v)

        flatq = wb["wq_s"].rearrange("a p d -> p a d")
        q0s, q0t = load_chunk(flatq[:, 0:4, :], "wq_s")
        q1s, q1t = load_chunk(flatq[:, 4:8, :], "wq_s")
        q_done_v = 0
        for cq in range(DC):
            bank = cq % 4
            slot, tok = (q0s, q0t) if cq < 4 else (q1s, q1t)
            ccol = (cq % 4) * 1024
            mats = [((lambda slot=slot, ccol=ccol, c=c: A[f"wbuf{slot}"][:, ccol + c * 128: ccol + (c + 1) * 128]),
                     (lambda c=c: A["fmT"][:, c * 512:(c + 1) * 512]))
                    for c in range(DC)]
            g_done = mm_group(bank, mats, waits=[(sV, fmT_v), chunk_wait(tok)])
            E.emit("vector", (lambda bank=bank, cq=cq: lambda eng: eng.tensor_scalar(
                A["scr4k"][:, 512:1024], P[bank][:, 0:512],
                A["bq_c"][:, cq:cq + 1], 0.0, ALU.add, ALU.min))(),
                waits=[(sP, g_done)], inc=sV)
            E.emit("vector", (lambda bank=bank, cq=cq: lambda eng: eng.tensor_scalar(
                A["scr4k"][:, 0:512], P[bank][:, 0:512],
                A["bq_c"][:, cq:cq + 1], 0.0, ALU.add, ALU.max))(), inc=sV)
            v_ab = E.sem_val(sV)
            rot_free[bank] = (sV, v_ab)
            E.emit("scalar", lambda eng: eng.activation(
                A["scr4k"][:, 512:1024], A["scr4k"][:, 512:1024], AF.Exp),
                waits=[(sV, v_ab)], inc=sS, flush=True)
            s_exp = E.sem_val(sS)
            E.emit("vector", (lambda cq=cq: lambda eng: eng.tensor_tensor(
                A["qbf"][:, cq * 512:(cq + 1) * 512], A["scr4k"][:, 0:512],
                A["scr4k"][:, 512:1024], ALU.add))(), waits=[(sS, s_exp)], inc=sV)
            q_done_v = E.sem_val(sV)
        chunk_consumed(q0t, q0s, sV, q_done_v)
        chunk_consumed(q1t, q1s, sV, q_done_v)

        zw = [(sV, q_done_v)]
        if tr_free[6] is not None:
            zw.append(tr_free[6])
        z_p = 0
        for c in range(DC):
            z_p = E.emit("tensor", (lambda c=c: lambda eng: eng.matmul(
                P[6][0:16, 0:512], A["ksml"][:, c * 16:(c + 1) * 16],
                A["qbf"][:, c * 512:(c + 1) * 512],
                start=(c == 0), stop=(c == DC - 1)))(),
                waits=zw if c == 0 else (), inc=sP if c == DC - 1 else None)
        E.emit("vector", lambda eng: eng.tensor_scalar(
            A["zsb"][:], P[6][0:16, 0:512], EPS_Z, 0.0, ALU.add, ALU.add),
            waits=[(sP, z_p)], inc=sV)
        E.emit("vector", lambda eng: eng.reciprocal(A["zsb"][:], A["zsb"][:]), inc=sV)
        zinv_v = E.sem_val(sV)
        tr_free[6] = (sV, zinv_v)

        o_done_v = 0
        for c in range(DC):
            bank = c % 4
            w = [(sV, max(q_done_v, kv_ready))]
            if rot_free[bank] is not None:
                w.append(rot_free[bank])
            E.emit("tensor", (lambda c=c, bank=bank: lambda eng: eng.matmul(
                P[bank][0:64, 0:512], A["kvsb"][0:64, c * 64:(c + 1) * 64],
                A["qbf"][0:64, c * 512:(c + 1) * 512], start=True, stop=True))(),
                waits=w)
            E.emit("tensor", (lambda c=c, bank=bank: lambda eng: eng.matmul(
                P[bank][64:128, 0:512], A["kvsb"][64:128, c * 64:(c + 1) * 64],
                A["qbf"][64:128, c * 512:(c + 1) * 512], start=True, stop=True))(),
                inc=sP)
            zbank = 6 + (c % 2)
            zslot = (c % 2) * 512
            zbw = [(sV, zinv_v), (qSel, sel_ld)]
            if tr_free[zbank] is not None:
                zbw.append(tr_free[zbank])
            zb_p = E.emit("tensor", (lambda c=c, zbank=zbank: lambda eng: eng.matmul(
                P[zbank][:, 0:512], A["selA"][:, c * 128:(c + 1) * 128], A["zsb"][:],
                start=True, stop=True))(), waits=zbw, inc=sP)
            ztile = "zbsb2" if c % 2 == 0 else "zbsbB"
            E.emit("vector", (lambda zbank=zbank, ztile=ztile: lambda eng: eng.tensor_copy(
                A[ztile][:, 0:512], P[zbank][:, 0:512]))(),
                waits=[(sP, zb_p)], inc=sV)
            zb_v = E.sem_val(sV)
            tr_free[zbank] = (sV, zb_v)
            E.emit("vector", (lambda c=c, bank=bank, ztile=ztile: lambda eng: eng.tensor_tensor(
                A["obf"][:, c * 512:(c + 1) * 512], P[bank][:, 0:512],
                A[ztile][:, 0:512], ALU.mult))(), inc=sV)
            o_done_v = E.sem_val(sV)
            rot_free[bank] = (sV, o_done_v)

        o0s, o0t = load_chunk(wb["wo_t"][:, 0:4, :], "wo_t")
        o1s, o1t = load_chunk(wb["wo_t"][:, 4:8, :], "wo_t")
        a_v = 0
        bank_i = 0
        for s in range(NS):
            for g in range(2):
                bank = bank_i % 4
                bank_i += 1
                mats = []
                for c in range(DC):
                    slot = o0s if c < 4 else o1s
                    cc = c % 4
                    mats.append((
                        (lambda c=c, s=s: A["obf"][:, c * 512 + s * 128: c * 512 + s * 128 + 128]),
                        (lambda slot=slot, cc=cc, g=g: A[f"wbuf{slot}"][:, cc * 1024 + g * 512: cc * 1024 + g * 512 + 512]),
                    ))
                g_done = mm_group(bank, mats,
                                  waits=[(sV, o_done_v), chunk_wait(o0t), chunk_wait(o1t)])
                off = s * 1024 + g * 512
                E.emit("vector", (lambda bank=bank, off=off: lambda eng: eng.tensor_tensor(
                    A["res1"][:, off:off + 512], A["res1"][:, off:off + 512],
                    P[bank][:, 0:512], ALU.add))(), waits=[(sP, g_done)], inc=sV)
                a_v = E.sem_val(sV)
                rot_free[bank] = (sV, a_v)
        chunk_consumed(o0t, o0s, sV, a_v)
        chunk_consumed(o1t, o1s, sV, a_v)
        for s in range(NS):
            E.emit("vector", (lambda s=s: lambda eng: eng.tensor_tensor(
                A["res1"][:, s * 1024:(s + 1) * 1024],
                A["res1"][:, s * 1024:(s + 1) * 1024], A["boB"][:], ALU.add))(), inc=sV)
        a_v = E.sem_val(sV)
        if stage == "a":
            return dbg_store(t, A["res1"], a_v)

        E.barrier("scalar", [(sV, a_v)])
        stats1 = ln_stats(A["res1"], A["res2"])
        n1_v = ln_apply(A["res1"], A["res2"], stats1)
        for s in range(NS):
            E.emit("vector", (lambda s=s: lambda eng: eng.tensor_tensor(
                A["res2"][:, s * 1024:(s + 1) * 1024],
                A["res2"][:, s * 1024:(s + 1) * 1024], A["g1B"][:], ALU.mult))(), inc=sV)
            E.emit("vector", (lambda s=s: lambda eng: eng.tensor_tensor(
                A["res2"][:, s * 1024:(s + 1) * 1024],
                A["res2"][:, s * 1024:(s + 1) * 1024], A["be1B"][:], ALU.add))(), inc=sV)
        x1_v = E.sem_val(sV)
        if stage == "x1":
            return dbg_store(t, A["res2"], x1_v)
        fmT1_v = tm_to_fm(A["res2"], "p2", [(sV, x1_v)], A["fmT"])

        flat1 = wb["w1_s"].rearrange("a p d -> p a d")
        g_v = 0
        w1tok = [None, None]
        for cq in range(FC):
            if cq % 4 == 0:
                i = cq // 4
                w1tok = list(load_chunk(flat1[:, i * 4:(i + 1) * 4, :], "w1_s"))
            bank = cq % 4
            slot, tok = w1tok
            ccol = (cq % 4) * 1024
            mats = [((lambda slot=slot, ccol=ccol, c=c: A[f"wbuf{slot}"][:, ccol + c * 128: ccol + (c + 1) * 128]),
                     (lambda c=c: A["fmT"][:, c * 512:(c + 1) * 512]))
                    for c in range(DC)]
            g_done = mm_group(bank, mats, waits=[(sV, fmT1_v), chunk_wait(tok)])
            E.emit("scalar", (lambda bank=bank, cq=cq: lambda eng: eng.activation(
                A["bigA"][:, cq * 512:(cq + 1) * 512], P[bank][:, 0:512], AF.Gelu,
                bias=A["b1_c"][:, cq:cq + 1], scale=1.0))(),
                waits=[(sP, g_done)], inc=sS)
            g_v = E.sem_val(sS)
            rot_free[bank] = (sS, g_v)
            if cq % 4 == 3:
                chunk_consumed(tok, slot, sS, g_v)
        E.emit("scalar", lambda eng: eng.drain(), inc=sS)
        g_v = E.sem_val(sS)

        y_v = 0
        for cq in range(DC):
            slot, tok = load_chunk(wb["w2_s"][cq], "w2_s")
            bank = cq % 4
            mats = [((lambda slot=slot, c=c: A[f"wbuf{slot}"][:, c * 128:(c + 1) * 128]),
                     (lambda c=c: A["bigA"][:, c * 512:(c + 1) * 512]))
                    for c in range(FC)]
            g_done = mm_group(bank, mats, waits=[(sS, g_v), chunk_wait(tok)])
            E.emit("vector", (lambda bank=bank, cq=cq: lambda eng: eng.tensor_scalar(
                A["scrF"][:, cq * 512:(cq + 1) * 512], P[bank][:, 0:512],
                A["b2_c"][:, cq:cq + 1], 0.0, ALU.add, ALU.add))(),
                waits=[(sP, g_done)], inc=sV)
            y_v = E.sem_val(sV)
            rot_free[bank] = (sV, y_v)
            chunk_consumed(tok, slot, sV, y_v)

        if stage == "yfm":
            return dbg_store(t, A["scrF"], y_v)
        r1_v = fm_to_tm(A["scrF"], "p2", [(sV, y_v)], A["res2"], add=True)
        if stage == "r1":
            return dbg_store(t, A["res2"], r1_v)

        E.barrier("scalar", [(sV, r1_v)])
        stats2 = ln_stats(A["res2"], A["res1"])
        n2_v = ln_apply(A["res2"], A["res2"], stats2)
        for s in range(NS):
            E.emit("vector", (lambda s=s: lambda eng: eng.tensor_tensor(
                A["res2"][:, s * 1024:(s + 1) * 1024],
                A["res2"][:, s * 1024:(s + 1) * 1024], A["g2B"][:], ALU.mult))(), inc=sV)
            E.emit("vector", (lambda s=s: lambda eng: eng.tensor_tensor(
                A["res2"][:, s * 1024:(s + 1) * 1024],
                A["res2"][:, s * 1024:(s + 1) * 1024], A["be2B"][:], ALU.add))(), inc=sV)
        blk_v = E.sem_val(sV)
        E.barrier("scalar", [(sV, blk_v)])
        stats3 = ln_stats(A["res2"], A["res1"])
        v_v = ln_apply(A["res2"], A["res2"], stats3)
        for s in range(NS):
            E.emit("vector", (lambda s=s: lambda eng: eng.tensor_tensor(
                A["res1"][:, s * 1024:(s + 1) * 1024],
                A["xbuf"][:, s * 1024:(s + 1) * 1024],
                A["res2"][:, s * 1024:(s + 1) * 1024], ALU.add))(), inc=sV)
        x2_v = E.sem_val(sV)
        if stage == "x2":
            return dbg_store(t, A["res1"], x2_v)

        E.barrier("scalar", [(sV, x2_v)])
        stats4 = ln_stats(A["res1"], A["res2"])
        ln2_v = ln_apply(A["res1"], A["res2"], stats4)
        if stage == "ln2t":
            return dbg_store(t, A["res2"], ln2_v)
        fmT2_v = tm_to_fm(A["res2"], "p2", [(sV, ln2_v)], A["fmT"])

        flatw1 = wb["fw1_s"].rearrange("a p d -> p a d")
        flatw2 = wb["fw2_s"].rearrange("a p d -> p a d")
        t1_v = 0
        gtoks = [None] * 4
        for cq in range(FC):
            if cq % 4 == 0:
                i = cq // 4
                gtoks[0], gtoks[1] = load_chunk(flatw1[:, i * 4:(i + 1) * 4, :], "fw1_s")
                gtoks[2], gtoks[3] = load_chunk(flatw2[:, i * 4:(i + 1) * 4, :], "fw2_s")
            bankA = (2 * cq) % 4
            bankB = (2 * cq + 1) % 4
            ccol = (cq % 4) * 1024
            mats1 = [((lambda slot=gtoks[0], ccol=ccol, c=c: A[f"wbuf{slot}"][:, ccol + c * 128: ccol + (c + 1) * 128]),
                      (lambda c=c: A["fmT"][:, c * 512:(c + 1) * 512]))
                     for c in range(DC)]
            mats2 = [((lambda slot=gtoks[2], ccol=ccol, c=c: A[f"wbuf{slot}"][:, ccol + c * 128: ccol + (c + 1) * 128]),
                      (lambda c=c: A["fmT"][:, c * 512:(c + 1) * 512]))
                     for c in range(DC)]
            gd1 = mm_group(bankA, mats1, waits=[(sV, fmT2_v), chunk_wait(gtoks[1])])
            gd2 = mm_group(bankB, mats2, waits=[(sV, fmT2_v), chunk_wait(gtoks[3])])
            E.emit("scalar", (lambda bankA=bankA, cq=cq: lambda eng: eng.activation(
                A["scr4k"][:, 0:512], P[bankA][:, 0:512], AF.Silu,
                bias=0.0, scale=A["s1_c"][:, cq:cq + 1]))(),
                waits=[(sP, gd1), (sV, t1_v)], inc=sS, flush=True)
            sil_v = E.sem_val(sS)
            rot_free[bankA] = (sS, sil_v)
            E.emit("vector", (lambda bankB=bankB, cq=cq: lambda eng: eng.scalar_tensor_tensor(
                A["bigA"][:, cq * 512:(cq + 1) * 512], P[bankB][:, 0:512],
                A["s2_c"][:, cq:cq + 1], A["scr4k"][:, 0:512], ALU.mult, ALU.mult))(),
                waits=[(sP, gd2), (sS, sil_v)], inc=sV)
            t1_v = E.sem_val(sV)
            rot_free[bankB] = (sV, t1_v)
            if cq % 4 == 3:
                chunk_consumed(gtoks[1], gtoks[0], sV, t1_v)
                chunk_consumed(gtoks[3], gtoks[2], sV, t1_v)

        def big_ff(wname, src, dst, scol, prev, first_extra=()):
            last_s = 0
            for cq in range(FC):
                slot, tok = load_chunk(wb[wname][cq], wname)
                bank = cq % 6
                mats = [((lambda slot=slot, c=c: A[f"wbuf{slot}"][:, c * 128:(c + 1) * 128]),
                         (lambda c=c, src=src: src[:, c * 512:(c + 1) * 512]))
                        for c in range(FC)]
                g_done = mm_group(bank, mats, waits=[prev, chunk_wait(tok)])
                ws = [(sP, g_done)] + (list(first_extra) if cq == 0 else [])
                E.emit("scalar", (lambda bank=bank, cq=cq, dst=dst, scol=scol:
                                  lambda eng: eng.activation(
                    dst[:, cq * 512:(cq + 1) * 512], P[bank][:, 0:512], AF.Gelu,
                    bias=0.0, scale=A[scol][:, cq:cq + 1]))(),
                    waits=ws, inc=sS)
                last_s = E.sem_val(sS)
                rot_free[bank] = (sS, last_s)
                chunk_consumed(tok, slot, sS, last_s)
            E.emit("scalar", lambda eng: eng.drain(), inc=sS)
            return E.sem_val(sS)

        t2_v = big_ff("h1_s", A["bigA"], A["bigB"], "sh1_c", (sV, t1_v),
                      first_extra=([state["store_done"]] if state.get("store_done") else []))
        t3_v = big_ff("h2_s", A["bigB"], A["bigA"], "sh2_c", (sS, t2_v))

        f_v = 0
        for cq in range(DC):
            slot, tok = load_chunk(wb["w3_s"][cq], "w3_s")
            bank = cq % 4
            mats = [((lambda slot=slot, c=c: A[f"wbuf{slot}"][:, c * 128:(c + 1) * 128]),
                     (lambda c=c: A["bigA"][:, c * 512:(c + 1) * 512]))
                    for c in range(FC)]
            g_done = mm_group(bank, mats, waits=[(sS, t3_v), chunk_wait(tok)])
            E.emit("vector", (lambda bank=bank, cq=cq: lambda eng: eng.tensor_scalar(
                A["scrF"][:, cq * 512:(cq + 1) * 512], P[bank][:, 0:512],
                A["s3_c"][:, cq:cq + 1], 0.0, ALU.mult, ALU.add))(),
                waits=[(sP, g_done)], inc=sV)
            f_v = E.sem_val(sV)
            rot_free[bank] = (sV, f_v)
            chunk_consumed(tok, slot, sV, f_v)
        state["sel_gate"] = (sV, f_v)

        if stage == "ffm":
            return dbg_store(t, A["scrF"], f_v)
        ftm_v = fm_to_tm(A["scrF"], "p2", [(sV, f_v)], A["res2"], add=False)
        E.barrier("scalar", [(sV, ftm_v)])
        stats5 = ln_stats(A["res2"], A["xbuf"])
        nf_v = ln_apply(A["res2"], A["res2"], stats5)
        for s in range(NS):
            E.emit("vector", (lambda s=s: lambda eng: eng.tensor_tensor(
                A["outstg"][:, s * 1024:(s + 1) * 1024],
                A["res1"][:, s * 1024:(s + 1) * 1024],
                A["res2"][:, s * 1024:(s + 1) * 1024], ALU.add))(), inc=sV)
        out_v = E.sem_val(sV)

        E.emit("scalar", (lambda t=t: lambda eng: eng.dma_start(
            out=out_tiled[t],
            in_=A["outstg"].rearrange("p (s d) -> p s d", d=D)))(),
            waits=[(sV, out_v)], inc=qX, amt=16)
        state["store_done"] = (qX, E.sem_val(qX))
        state["xbuf_free"] = (sV, stats5)
        state["res_free"] = (sV, out_v)

    for t in range(n_tiles):
        pass2_tile(t)

    E.barrier("scalar", [(qX, E.sem_val(qX))])
    return finish(nc, E, stack)


def finish(nc, E, stack):
    with nc.Block() as blk:
        def make_body(eng_name):
            def body(eng):
                for fn, waits, inc in E.ev[eng_name]:
                    for sem, val in waits:
                        eng.wait_ge(sem, val)
                    if fn is None:
                        assert inc is None
                        continue
                    inst = fn(eng)
                    if inc is not None:
                        inst.then_inc(inc[0], inc[1])
            return body
        blk.sync(make_body("sync"))
        blk.gpsimd(make_body("gpsimd"))
        blk.scalar(make_body("scalar"))
        blk.vector(make_body("vector"))
        blk.tensor(make_body("tensor"))
    stack.close()
    return nc


def _prep_inputs(ins):
    f32 = np.float32

    def fm_swizzle(w_in_out, co_n, ci_n):
        a = np.asarray(w_in_out, f32).reshape(ci_n, 128, co_n, 128)
        return np.ascontiguousarray(a.transpose(2, 1, 0, 3).reshape(co_n, 128, ci_n * 128))

    def tm_swizzle(w_in_out):
        a = np.asarray(w_in_out, f32).reshape(8, 128, -1)
        return np.ascontiguousarray(a.transpose(1, 0, 2))

    def colmat(v, n):
        return np.ascontiguousarray(np.asarray(v, f32).reshape(n, 128).T)

    d = {}
    d["wk_t"] = tm_swizzle(ins["L0_attn_Wk"])
    d["wv_t"] = tm_swizzle(ins["L0_attn_Wv"])
    d["wo_t"] = tm_swizzle(ins["L0_attn_Wo"])
    d["wq_s"] = fm_swizzle(ins["L0_attn_Wq"], 8, 8)
    d["w1_s"] = fm_swizzle(ins["L0_attn_W1"], 32, 8)
    d["w2_s"] = fm_swizzle(ins["L0_attn_W2"], 8, 32)
    d["fw1_s"] = fm_swizzle(np.asarray(ins["L0_ffn_w1_w"], f32).T, 32, 8)
    d["fw2_s"] = fm_swizzle(np.asarray(ins["L0_ffn_w2_w"], f32).T, 32, 8)
    d["h1_s"] = fm_swizzle(np.asarray(ins["L0_ffn_h1_w"], f32).T, 32, 32)
    d["h2_s"] = fm_swizzle(np.asarray(ins["L0_ffn_h2_w"], f32).T, 32, 32)
    d["w3_s"] = fm_swizzle(np.asarray(ins["L0_ffn_w3_w"], f32).T, 8, 32)
    for k, src in [("bk", "L0_attn_bk"), ("bv", "L0_attn_bv"), ("bo", "L0_attn_bo"),
                   ("g1", "L0_attn_g1"), ("be1", "L0_attn_be1"),
                   ("g2", "L0_attn_g2"), ("be2", "L0_attn_be2")]:
        d[k] = np.ascontiguousarray(np.asarray(ins[src], f32))
    d["bq_c"] = colmat(ins["L0_attn_bq"], 8)
    d["b1_c"] = colmat(ins["L0_attn_b1"], 32)
    d["b2_c"] = colmat(ins["L0_attn_b2"], 8)
    d["s1_c"] = colmat(ins["L0_ffn_w1_s"], 32)
    d["s2_c"] = colmat(ins["L0_ffn_w2_s"], 32)
    d["sh1_c"] = colmat(ins["L0_ffn_h1_s"], 32)
    d["sh2_c"] = colmat(ins["L0_ffn_h2_s"], 32)
    d["s3_c"] = colmat(ins["L0_ffn_w3_s"], 8)
    d["id_f32"] = np.eye(128, dtype=f32)
    sel = np.zeros((16, 1024), f32)
    for c in range(8):
        sel[2 * c, c * 128:c * 128 + 64] = 1.0
        sel[2 * c + 1, c * 128 + 64:c * 128 + 128] = 1.0
    d["sel_f32"] = sel
    return d


def kernel(**inputs):
    shared = _prep_inputs(inputs)
    x = np.asarray(inputs["L0_attn_src"], np.float32)
    nc = build("full")
    in_maps = []
    for b in range(8):
        m = dict(shared)
        m["x"] = np.ascontiguousarray(x[b])
        in_maps.append(m)
    res = run_bass_kernel_spmd(nc, in_maps, core_ids=list(range(8)))
    out = np.stack([np.asarray(r["out"]) for r in res.results], axis=0)
    return out.astype(np.float32)


if __name__ == "__main__":
    print("building program...")
    nc = build("full")
    print("n_instructions:", nc.n_instructions())


# version 11
# speedup vs baseline: 1.1448x; 1.0633x over previous
import contextlib
import numpy as np
import concourse.bass as bass
import concourse.mybir as mybir
from concourse.bass_utils import run_bass_kernel_spmd

F32 = mybir.dt.float32
BF16 = mybir.dt.bfloat16
AF = mybir.ActivationFunctionType
ALU = mybir.AluOpType
AX = mybir.AxisListType

S, D, H, DH, FF = 4096, 1024, 16, 64, 4096
T = 512
NT = S // T
NS = T // 128
DC = D // 128
FC = FF // 128
EPS_LN = 1e-5
EPS_Z = 1e-6
KB = 1024

ARENA = 16512
OFF = dict(xbuf=ARENA, res1=ARENA + 16 * KB, res2=ARENA + 32 * KB,
           fmT=ARENA + 48 * KB, scrF=ARENA + 56 * KB, bigA=ARENA + 72 * KB,
           bigB=ARENA + 104 * KB, wbuf=ARENA + 136 * KB, consts=ARENA + 168 * KB)


def _roundup(x, a):
    return (x + a - 1) // a * a


class Emitter:
    def __init__(self):
        self.ev = {k: [] for k in ("sync", "gpsimd", "scalar", "vector", "tensor")}
        self.ctr = {}
        self.sems = {}
        self.bounce_cache = {}

    def sem_val(self, sem):
        return self.ctr.get(sem.name, 0)

    def bounce(self, sem, val):
        key = (sem.name, val)
        if key in self.bounce_cache:
            return self.bounce_cache[key]
        via = "scalar" if sem.name == "sV" else "vector"
        via_sem = self.sems["sS"] if via == "scalar" else self.sems["sV"]
        self.ev[via].append((None, [(sem, val)], None))
        self.ctr[via_sem.name] = self.ctr.get(via_sem.name, 0) + 1
        self.ev[via].append((lambda eng_: eng_.drain(), [], (via_sem, 1)))
        out = (via_sem, self.ctr[via_sem.name])
        self.bounce_cache[key] = out
        return out

    def emit(self, eng, fn, waits=(), inc=None, amt=1, flush=False):
        w = []
        for (sem, val) in waits:
            if val is None or val <= 0:
                continue
            assert val <= self.ctr.get(sem.name, 0), \
                f"wait {sem.name}>={val} but only {self.ctr.get(sem.name, 0)} emitted"
            w.append((sem, val))
        if inc is not None and amt == 1 and eng in ("vector", "scalar", "gpsimd"):
            self.ev[eng].append((fn, w, None))
            self.ctr[inc.name] = self.ctr.get(inc.name, 0) + amt
            self.ev[eng].append((lambda eng_: eng_.drain(), [], (inc, amt)))
            return self.ctr.get(inc.name, 0)
        incv = None
        if inc is not None:
            self.ctr[inc.name] = self.ctr.get(inc.name, 0) + amt
            incv = (inc, amt)
        self.ev[eng].append((fn, w, incv))
        return self.ctr.get(inc.name, 0) if inc is not None else None

    def barrier(self, eng, waits):
        self.emit(eng, None, waits=waits)


def build(stage="full", n_tiles=NT):
    nc = bass.Bass("TRN2", target_bir_lowering=False, debug=False)

    x_h = nc.dram_tensor("x", [S, D], F32, kind="ExternalInput").ap()
    out_h = nc.dram_tensor("out", [S, D], F32, kind="ExternalOutput").ap()

    wf = {}
    for name, shape in [
        ("wk_t", [128, DC, D]), ("wv_t", [128, DC, D]), ("wo_t", [128, DC, D]),
        ("wq_s", [DC, 128, D]),
        ("w1_s", [FC, 128, D]), ("w2_s", [DC, 128, FF]),
        ("fw1_s", [FC, 128, D]), ("fw2_s", [FC, 128, D]),
        ("h1_s", [FC, 128, FF]), ("h2_s", [FC, 128, FF]),
        ("w3_s", [DC, 128, FF]),
    ]:
        wf[name] = nc.dram_tensor(name, shape, F32, kind="ExternalInput").ap()
    wb = {name: nc.dram_tensor(name + "_b", list(ap.shape), BF16, kind="Internal").ap()
          for name, ap in wf.items()}

    vecs = {name: nc.dram_tensor(name, [D], F32, kind="ExternalInput").ap()
            for name in ("bk", "bv", "bo", "g1", "be1", "g2", "be2")}
    cols_h = {}
    for name, w in [("bq_c", DC), ("b2_c", DC), ("s3_c", DC), ("b1_c", FC),
                    ("s1_c", FC), ("s2_c", FC), ("sh1_c", FC), ("sh2_c", FC)]:
        cols_h[name] = nc.dram_tensor(name, [128, w], F32, kind="ExternalInput").ap()
    id_h = nc.dram_tensor("id_f32", [128, 128], F32, kind="ExternalInput").ap()
    sel_h = nc.dram_tensor("sel_f32", [16, 1024], F32, kind="ExternalInput").ap()

    A = {}

    def sba(name, shape, dtype, off):
        A[name] = nc.alloc_sbuf_tensor_at(name, shape, dtype, offset=off).ap()

    sba("xbuf", [128, 4096], F32, OFF["xbuf"])
    sba("res1", [128, 4096], F32, OFF["res1"])
    sba("res2", [128, 4096], F32, OFF["res2"])
    sba("fmT", [128, 4096], BF16, OFF["fmT"])
    sba("kbf", [128, 4096], BF16, OFF["scrF"])
    sba("vbf", [128, 4096], BF16, OFF["scrF"] + 8 * KB)
    sba("qbf", [128, 4096], BF16, OFF["scrF"])
    sba("obf", [128, 4096], BF16, OFF["scrF"] + 8 * KB)
    sba("scrF", [128, 4096], F32, OFF["scrF"])
    sba("bigA", [128, 16384], BF16, OFF["bigA"])
    sba("bigB", [128, 16384], BF16, OFF["bigB"])
    sba("selA", [16, 1024], F32, OFF["bigA"] + 28 * KB)
    sba("zbsbB", [128, 512], F32, OFF["bigB"] + 30 * KB)
    sba("outstg", [128, 4096], F32, OFF["bigB"])
    for i in range(4):
        sba(f"wbuf{i}", [128, 4096], BF16, OFF["wbuf"] + i * 8 * KB)

    co = OFF["consts"]

    def calloc(name, shape, dtype):
        nonlocal co
        n = int(np.prod(shape[1:]))
        sba(name, shape, dtype, co)
        co += _roundup(n * (2 if dtype == BF16 else 4), 32)

    for nm in ("bkB", "bvB", "boB", "g1B", "be1B", "g2B", "be2B"):
        calloc(nm, [128, 1024], F32)
    for nm, w in [("bq_c", DC), ("b2_c", DC), ("s3_c", DC), ("b1_c", FC),
                  ("s1_c", FC), ("s2_c", FC), ("sh1_c", FC), ("sh2_c", FC)]:
        calloc(nm, [128, w], F32)
    calloc("idf", [128, 128], F32)
    calloc("kvsb", [128, 512], BF16)
    calloc("ksml", [128, 128], BF16)
    calloc("zsb", [16, 512], F32)
    calloc("zbsb2", [128, 512], F32)
    calloc("ones", [128, 1], BF16)
    calloc("stats", [128, 32], F32)
    calloc("pad8", [128, 8], F32)
    calloc("scr4k", [128, 1024], F32)
    assert co <= 229344, f"consts overflow: {co}"

    P = [nc.alloc_psum_tensor(f"p{i}", [128, 512], F32).ap() for i in range(8)]

    E = Emitter()
    stack = contextlib.ExitStack()
    qW = [stack.enter_context(nc.semaphore(f"qW{i}")) for i in range(4)]
    qK = stack.enter_context(nc.semaphore("qK"))
    qSel = stack.enter_context(nc.semaphore("qSel"))
    qX = stack.enter_context(nc.semaphore("qX"))
    qC = stack.enter_context(nc.semaphore("qC"))
    sP = stack.enter_context(nc.semaphore("sP"))
    sV = stack.enter_context(nc.semaphore("sV"))
    sS = stack.enter_context(nc.semaphore("sS"))
    E.sems = {"sV": sV, "sS": sS}

    for nm, dst in [("bk", "bkB"), ("bv", "bvB"), ("bo", "boB"), ("g1", "g1B"),
                    ("be1", "be1B"), ("g2", "g2B"), ("be2", "be2B")]:
        E.emit("sync", (lambda d=A[dst], s=vecs[nm]: lambda eng: eng.dma_start(
            out=d[:], in_=s.unsqueeze(0).partition_broadcast(128)))(), inc=qK, amt=16)
    for nm in ("bq_c", "b2_c", "s3_c", "b1_c", "s1_c", "s2_c", "sh1_c", "sh2_c"):
        E.emit("sync", (lambda d=A[nm], s=cols_h[nm]: lambda eng: eng.dma_start(
            out=d[:], in_=s))(), inc=qK, amt=16)
    E.emit("sync", lambda eng: eng.dma_start(out=A["idf"][:], in_=id_h), inc=qK, amt=16)
    consts_loaded = E.sem_val(qK)

    cast_done = {}
    for name in ("wk_t", "wv_t", "wq_s", "wo_t", "w1_s", "w2_s",
                 "fw1_s", "fw2_s", "h1_s", "h2_s", "w3_s"):
        src, dst = wf[name], wb[name]
        n0 = src.shape[0]
        pieces = 4 if int(np.prod(src.shape)) * 4 > 20e6 else 1
        step = n0 // pieces
        for i in range(pieces):
            prev = E.sem_val(qC)
            E.emit("gpsimd", (lambda d=dst, s=src, a=i * step, b=(i + 1) * step:
                              lambda eng: eng.dma_start(out=d[a:b], in_=s[a:b]))(),
                   waits=[(qC, prev)], inc=qC, amt=16)
        cast_done[name] = E.sem_val(qC)

    ring = {"idx": 0, "loadedv": {}, "freed": [None] * 4}

    def load_chunk(dram_ap, matrix):
        slot = ring["idx"] % 4
        waits = [(qC, cast_done[matrix])]
        if ring["freed"][slot] is not None:
            waits.append(ring["freed"][slot])
        if ring.get("extra"):
            waits.append(ring["extra"])
        if len(dram_ap.shape) == 3:
            dview = A[f"wbuf{slot}"].rearrange("p (a d) -> p a d", d=dram_ap.shape[2])
        else:
            dview = A[f"wbuf{slot}"][:]
        E.emit("sync", (lambda d=dview, s=dram_ap:
                        lambda eng: eng.dma_start(out=d, in_=s))(),
               waits=waits, inc=qW[slot], amt=16)
        tok = ring["idx"]
        ring["loadedv"][tok] = (qW[slot], E.sem_val(qW[slot]))
        ring["idx"] += 1
        return slot, tok

    def chunk_wait(tok):
        return ring["loadedv"][tok]

    def chunk_consumed(tok, slot, sem, val):
        ring["freed"][slot] = (sem, val)

    def ln_stats(src, scratch, src_ready=None, base=0):
        st = A["stats"]
        b = base
        for s in range(NS):
            ws = list(src_ready or []) if s == 0 else []
            E.emit("scalar", (lambda s=s, scr=scratch: lambda eng: eng.activation(
                scr[:, s * 1024:(s + 1) * 1024], src[:, s * 1024:(s + 1) * 1024],
                AF.Square))(), waits=ws,
                inc=sS if s == NS - 1 else None, flush=(s == NS - 1))
        sq_done = E.sem_val(sS)
        for s in range(NS):
            E.emit("vector", (lambda s=s, b=b: lambda eng: eng.reduce_sum(
                st[:, b + s:b + s + 1], src[:, s * 1024:(s + 1) * 1024], axis=AX.X))(),
                inc=sV)
        for s in range(NS):
            E.emit("vector", (lambda s=s, b=b, scr=scratch: lambda eng: eng.reduce_sum(
                st[:, b + 4 + s:b + 5 + s], scr[:, s * 1024:(s + 1) * 1024], axis=AX.X))(),
                waits=[(sS, sq_done)] if s == 0 else (), inc=sV)
        for j in range(3):
            E.emit("vector", (lambda j=j: lambda eng: eng.memset(A["pad8"][:], float(j)))())
        E.emit("vector", (lambda b=b: lambda eng: eng.tensor_scalar(
            st[:, b:b + 4], st[:, b:b + 4], 1.0 / D, 0.0, ALU.mult, ALU.add))(), inc=sV)
        E.emit("vector", (lambda b=b: lambda eng: eng.tensor_scalar(
            st[:, b + 4:b + 8], st[:, b + 4:b + 8], 1.0 / D, EPS_LN, ALU.mult, ALU.add))(), inc=sV)
        E.emit("vector", (lambda b=b: lambda eng: eng.tensor_tensor(
            st[:, b + 8:b + 12], st[:, b:b + 4], st[:, b:b + 4], ALU.mult))(), inc=sV)
        E.emit("vector", (lambda b=b: lambda eng: eng.tensor_tensor(
            st[:, b + 4:b + 8], st[:, b + 4:b + 8], st[:, b + 8:b + 12], ALU.subtract))(), inc=sV)
        v_var = E.sem_val(sV)
        E.emit("scalar", (lambda b=b: lambda eng: eng.activation(
            st[:, b + 4:b + 8], st[:, b + 4:b + 8], AF.Sqrt))(),
            waits=[(sV, v_var)], inc=sS, flush=True)
        s_sqrt = E.sem_val(sS)
        E.emit("vector", (lambda b=b: lambda eng: eng.reciprocal(
            st[:, b + 4:b + 8], st[:, b + 4:b + 8]))(),
               waits=[(sS, s_sqrt)], inc=sV)
        return E.sem_val(sV)

    def ln_apply(src, dst, stats_ready, base=0, extra_waits=()):
        st = A["stats"]
        for s in range(NS):
            w = ([(sV, stats_ready)] + list(extra_waits)) if s == 0 else []
            E.emit("vector", (lambda s=s, b=base: lambda eng: eng.tensor_scalar(
                dst[:, s * 1024:(s + 1) * 1024], src[:, s * 1024:(s + 1) * 1024],
                st[:, b + s:b + s + 1], st[:, b + 4 + s:b + 5 + s],
                ALU.subtract, ALU.mult))(), waits=w, inc=sV)
        return E.sem_val(sV)

    tr_banks = {"p1": (4, 5), "p2": (6, 7)}
    tr_free = {4: None, 5: None, 6: None, 7: None}
    rot_free = {i: None for i in range(6)}

    def transpose_sweep(src, which, ready_w, evac):
        b0, b1 = tr_banks[which]
        last_v = 0
        for s in range(NS):
            for half in range(2):
                bank = b0 if half == 0 else b1
                waits = list(ready_w)
                if tr_free[bank] is not None:
                    waits.append(tr_free[bank])
                for j in range(4):
                    c = half * 4 + j
                    E.emit("tensor", (lambda s=s, c=c, j=j, bank=bank, src=src:
                                      lambda eng: eng.transpose(
                        P[bank][:, j * 128:(j + 1) * 128],
                        src[:, ln_tr_in(s, c)], A["idf"][:]))(),
                        waits=waits if j == 0 else (),
                        inc=sP if j == 3 else None)
                p_done = E.sem_val(sP)
                last_v = evac(s, half, bank, p_done)
                tr_free[bank] = (sV, last_v)
        return last_v

    def ln_tr_in(s, c):
        return slice(s * 1024 + c * 128, s * 1024 + (c + 1) * 128)

    def tm_to_fm(src, which, ready_w, dst):
        def evac(s, half, bank, p_done):
            E.emit("vector", (lambda s=s, half=half, bank=bank, d=dst:
                              lambda eng: eng.tensor_copy(
                d.rearrange("p (c x) -> p c x", x=512)[:, half * 4:half * 4 + 4,
                                                       s * 128:s * 128 + 128],
                P[bank].rearrange("p (j x) -> p j x", x=128)))(),
                waits=[(sP, p_done)], inc=sV)
            return E.sem_val(sV)
        return transpose_sweep(src, which, ready_w, evac)

    def fm_to_tm(src_fm, which, ready_w, dst_tm, add):
        b0, b1 = tr_banks[which]
        last_v = 0
        for s in range(NS):
            for half in range(2):
                bank = b0 if half == 0 else b1
                waits = list(ready_w)
                if tr_free[bank] is not None:
                    waits.append(tr_free[bank])
                for j in range(4):
                    c = half * 4 + j
                    E.emit("tensor", (lambda s=s, c=c, j=j, bank=bank, src=src_fm:
                                      lambda eng: eng.transpose(
                        P[bank][:, j * 128:(j + 1) * 128],
                        src[:, c * 512 + s * 128: c * 512 + s * 128 + 128],
                        A["idf"][:]))(),
                        waits=waits if j == 0 else (),
                        inc=sP if j == 3 else None)
                p_done = E.sem_val(sP)
                dview = dst_tm.rearrange("p (s c x) -> p s c x", s=NS, x=128)
                if add:
                    E.emit("vector", (lambda s=s, half=half, bank=bank, dv=dview:
                                      lambda eng: eng.tensor_tensor(
                        dv[:, s, half * 4:half * 4 + 4, :],
                        dv[:, s, half * 4:half * 4 + 4, :],
                        P[bank].rearrange("p (j x) -> p j x", x=128), ALU.add))(),
                        waits=[(sP, p_done)], inc=sV)
                else:
                    E.emit("vector", (lambda s=s, half=half, bank=bank, dv=dview:
                                      lambda eng: eng.tensor_copy(
                        dv[:, s, half * 4:half * 4 + 4, :],
                        P[bank].rearrange("p (j x) -> p j x", x=128)))(),
                        waits=[(sP, p_done)], inc=sV)
                last_v = E.sem_val(sV)
                tr_free[bank] = (sV, last_v)
        return last_v

    def mm_group(bank, matmuls, waits=()):
        n = len(matmuls)
        for i, fn in enumerate(matmuls):
            w = []
            if i == 0:
                if rot_free[bank] is not None:
                    w.append(rot_free[bank])
                w += list(waits)
            E.emit("tensor", (lambda fn=fn, i=i, bank=bank, n=n:
                              lambda eng: eng.matmul(
                P[bank][:, 0:512], fn[0](), fn[1](),
                start=(i == 0), stop=(i == n - 1)))(),
                waits=w, inc=sP if i == n - 1 else None)
        return E.sem_val(sP)

    x_tiled = x_h.rearrange("(t s p) d -> t p s d", p=128, s=NS)
    out_tiled = out_h.rearrange("(t s p) d -> t p s d", p=128, s=NS)
    xbuf_3d = A["xbuf"].rearrange("p (s d) -> p s d", d=D)

    state = {"xbuf_free": None, "store_done": None, "res_free": None, "kv_first": True,
             "ks_first": True}

    def pass1_tile(t):
        waits = []
        if state["xbuf_free"] is not None:
            waits.append(state["xbuf_free"])
        E.emit("scalar", (lambda t=t: lambda eng: eng.dma_start(
            out=xbuf_3d[:], in_=x_tiled[t]))(), waits=waits, inc=qX, amt=16)
        x_ld = E.sem_val(qX)
        E.barrier("vector", [(qX, x_ld), (qK, consts_loaded if t == 0 else 0)])
        E.barrier("scalar", [(qX, x_ld)])
        stats_v = ln_stats(A["xbuf"], A["res2"])
        u_v = ln_apply(A["xbuf"], A["res1"], stats_v)
        state["xbuf_free"] = (sV, u_v)
        fmT_v = tm_to_fm(A["res1"], "p1", [(sV, u_v)], A["fmT"])

        def proj_tm(wname, dst_bf, is_k):
            s0, t0c = load_chunk(wb[wname][:, 0:4, :], wname)
            s1, t1c = load_chunk(wb[wname][:, 4:8, :], wname)
            lastV = 0
            bank_i = 0
            for s in range(NS):
                for g in range(2):
                    bank = bank_i % 4
                    bank_i += 1
                    mats = []
                    for c in range(DC):
                        slot = s0 if c < 4 else s1
                        cc = c % 4
                        mats.append((
                            (lambda c=c, s=s: A["fmT"][:, c * 512 + s * 128: c * 512 + s * 128 + 128]),
                            (lambda slot=slot, cc=cc, g=g: A[f"wbuf{slot}"][:, cc * 1024 + g * 512: cc * 1024 + g * 512 + 512]),
                        ))
                    g_done = mm_group(bank, mats,
                                      waits=[(sV, fmT_v), chunk_wait(t0c), chunk_wait(t1c)])
                    off = s * 1024 + g * 512
                    if is_k:
                        E.emit("vector", (lambda bank=bank, g=g: lambda eng: eng.tensor_tensor(
                            A["scr4k"][:, 0:512], P[bank][:, 0:512],
                            A["bkB"][:, g * 512:(g + 1) * 512], ALU.add))(),
                            waits=[(sP, g_done)], inc=sV)
                        v_tt = E.sem_val(sV)
                        rot_free[bank] = (sV, v_tt)
                        E.emit("vector", lambda eng: eng.tensor_scalar(
                            A["scr4k"][:, 512:1024], A["scr4k"][:, 0:512],
                            0.0, 0.0, ALU.min, ALU.add), inc=sV)
                        v_min = E.sem_val(sV)
                        E.emit("scalar", lambda eng: eng.activation(
                            A["scr4k"][:, 512:1024], A["scr4k"][:, 512:1024], AF.Exp),
                            waits=[(sV, v_min)], inc=sS, flush=True)
                        s_exp = E.sem_val(sS)
                        E.emit("vector", (lambda off=off, dst=dst_bf: lambda eng: eng.scalar_tensor_tensor(
                            dst[:, off:off + 512], A["scr4k"][:, 0:512], 0.0,
                            A["scr4k"][:, 512:1024], ALU.max, ALU.add))(),
                            waits=[(sS, s_exp)], inc=sV)
                        lastV = E.sem_val(sV)
                    else:
                        E.emit("vector", (lambda bank=bank, g=g, off=off, dst=dst_bf:
                                          lambda eng: eng.tensor_tensor(
                            dst[:, off:off + 512], P[bank][:, 0:512],
                            A["bvB"][:, g * 512:(g + 1) * 512], ALU.add))(),
                            waits=[(sP, g_done)], inc=sV)
                        lastV = E.sem_val(sV)
                        rot_free[bank] = (sV, lastV)
            chunk_consumed(t0c, s0, sV, lastV)
            chunk_consumed(t1c, s1, sV, lastV)
            return lastV

        k_done = proj_tm("wk_t", A["kbf"], True)
        v_done = proj_tm("wv_t", A["vbf"], False)

        if t == 0:
            E.emit("vector", lambda eng: eng.memset(A["ones"][:], 1.0), inc=sV)
        ones_v = E.sem_val(sV)

        kv_p = 0
        for s in range(NS):
            for h in range(H):
                pb = (h % 2) * 64
                c = h // 2
                w = [(sV, max(k_done, v_done, ones_v))] if (s == 0 and h == 0) else []
                last = (t == n_tiles - 1 and s == NS - 1 and h == H - 1)
                E.emit("tensor", (lambda s=s, h=h, pb=pb, c=c, last=last: lambda eng: eng.matmul(
                    P[6][pb:pb + 64, c * 64:(c + 1) * 64],
                    A["kbf"][:, s * 1024 + h * 64: s * 1024 + (h + 1) * 64],
                    A["vbf"][:, s * 1024 + h * 64: s * 1024 + (h + 1) * 64],
                    start=state["kv_first"], stop=last))(), waits=w)
                state["kv_first"] = False
                kv_p = E.emit("tensor", (lambda s=s, h=h, pb=pb, c=c, last=last: lambda eng: eng.matmul(
                    P[7][pb:pb + 64, c:c + 1],
                    A["kbf"][:, s * 1024 + h * 64: s * 1024 + (h + 1) * 64],
                    A["ones"][:, 0:1],
                    start=state["ks_first"], stop=last))(),
                    inc=sP if last else None)
                state["ks_first"] = False
        return kv_p

    for t in range(n_tiles):
        kv_p = pass1_tile(t)

    E.barrier("scalar", [(sP, kv_p)])
    E.emit("scalar", lambda eng: eng.drain(), inc=sS)
    kv_sS = E.sem_val(sS)
    E.emit("vector", lambda eng: eng.tensor_copy(A["kvsb"][:], P[6][:]),
           waits=[(sP, kv_p), (sS, kv_sS)], inc=sV)
    E.emit("vector", lambda eng: eng.memset(A["ksml"][:], 0.0), inc=sV)
    for h in range(H):
        pb = (h % 2) * 64
        c = h // 2
        E.emit("vector", (lambda h=h, pb=pb, c=c: lambda eng: eng.tensor_copy(
            A["ksml"][pb:pb + 64, c * 16 + h: c * 16 + h + 1],
            P[7][pb:pb + 64, c:c + 1]))(), inc=sV)
    kv_ready = E.sem_val(sV)
    tr_free[6] = (sV, kv_ready)
    tr_free[7] = (sV, kv_ready)
    rot_free[4] = tr_free[4]
    rot_free[5] = tr_free[5]

    def dbg_store(t, tile, v):
        E.emit("scalar", (lambda t=t, tile=tile: lambda eng: eng.dma_start(
            out=out_tiled[t], in_=tile.rearrange("p (s d) -> p s d", d=D)))(),
               waits=[(sV, v)], inc=qX, amt=16)
        state["store_done"] = (qX, E.sem_val(qX))
        state["xbuf_free"] = (sV, v)
        state["sel_gate"] = state["store_done"]

    def pass2_tile(t):
        import os
        if os.environ.get("SERIAL_LOADS"):
            ring["extra"] = state["store_done"]
        waits = []
        if state["xbuf_free"] is not None:
            waits.append(state["xbuf_free"])
        if state["store_done"] is not None:
            waits.append(state["store_done"])
        selw = [] if state.get("sel_gate") is None else [state["sel_gate"]]
        E.emit("sync", lambda eng: eng.dma_start(out=A["selA"][:], in_=sel_h),
               waits=selw, inc=qSel, amt=16)
        sel_ld = E.sem_val(qSel)
        if state.get("early") is not None:
            stats_v, sd = state.pop("early")
            u_v = ln_apply(A["xbuf"], A["res1"], stats_v, base=16,
                           extra_waits=[sd, state["res_free"]])
        else:
            E.emit("scalar", (lambda t=t: lambda eng: eng.dma_start(
                out=xbuf_3d[:], in_=x_tiled[t]))(), waits=waits, inc=qX, amt=16)
            x_ld = E.sem_val(qX)
            E.barrier("vector", [(qX, x_ld)])
            E.barrier("scalar", [(qX, x_ld)])
            stats_v = ln_stats(A["xbuf"], A["res2"])
            u_v = ln_apply(A["xbuf"], A["res1"], stats_v)
        fmT_v = tm_to_fm(A["res1"], "p2", [(sV, u_v)], A["fmT"])
        if stage == "u":
            return dbg_store(t, A["res1"], u_v)

        flatq = wb["wq_s"].rearrange("a p d -> p a d")
        q0s, q0t = load_chunk(flatq[:, 0:4, :], "wq_s")
        q1s, q1t = load_chunk(flatq[:, 4:8, :], "wq_s")
        q_done_v = 0
        for cq in range(DC):
            bank = cq % 4
            slot, tok = (q0s, q0t) if cq < 4 else (q1s, q1t)
            ccol = (cq % 4) * 1024
            mats = [((lambda slot=slot, ccol=ccol, c=c: A[f"wbuf{slot}"][:, ccol + c * 128: ccol + (c + 1) * 128]),
                     (lambda c=c: A["fmT"][:, c * 512:(c + 1) * 512]))
                    for c in range(DC)]
            g_done = mm_group(bank, mats, waits=[(sV, fmT_v), chunk_wait(tok)])
            E.emit("vector", (lambda bank=bank, cq=cq: lambda eng: eng.tensor_scalar(
                A["scr4k"][:, 512:1024], P[bank][:, 0:512],
                A["bq_c"][:, cq:cq + 1], 0.0, ALU.add, ALU.min))(),
                waits=[(sP, g_done)], inc=sV)
            E.emit("vector", (lambda bank=bank, cq=cq: lambda eng: eng.tensor_scalar(
                A["scr4k"][:, 0:512], P[bank][:, 0:512],
                A["bq_c"][:, cq:cq + 1], 0.0, ALU.add, ALU.max))(), inc=sV)
            v_ab = E.sem_val(sV)
            rot_free[bank] = (sV, v_ab)
            E.emit("scalar", lambda eng: eng.activation(
                A["scr4k"][:, 512:1024], A["scr4k"][:, 512:1024], AF.Exp),
                waits=[(sV, v_ab)], inc=sS, flush=True)
            s_exp = E.sem_val(sS)
            E.emit("vector", (lambda cq=cq: lambda eng: eng.tensor_tensor(
                A["qbf"][:, cq * 512:(cq + 1) * 512], A["scr4k"][:, 0:512],
                A["scr4k"][:, 512:1024], ALU.add))(), waits=[(sS, s_exp)], inc=sV)
            q_done_v = E.sem_val(sV)
        chunk_consumed(q0t, q0s, sV, q_done_v)
        chunk_consumed(q1t, q1s, sV, q_done_v)

        zw = [(sV, q_done_v)]
        if tr_free[6] is not None:
            zw.append(tr_free[6])
        z_p = 0
        for c in range(DC):
            z_p = E.emit("tensor", (lambda c=c: lambda eng: eng.matmul(
                P[6][0:16, 0:512], A["ksml"][:, c * 16:(c + 1) * 16],
                A["qbf"][:, c * 512:(c + 1) * 512],
                start=(c == 0), stop=(c == DC - 1)))(),
                waits=zw if c == 0 else (), inc=sP if c == DC - 1 else None)
        E.emit("vector", lambda eng: eng.tensor_scalar(
            A["zsb"][:], P[6][0:16, 0:512], EPS_Z, 0.0, ALU.add, ALU.add),
            waits=[(sP, z_p)], inc=sV)
        E.emit("vector", lambda eng: eng.reciprocal(A["zsb"][:], A["zsb"][:]), inc=sV)
        zinv_v = E.sem_val(sV)
        tr_free[6] = (sV, zinv_v)

        o_done_v = 0
        for c in range(DC):
            bank = c % 4
            w = [(sV, max(q_done_v, kv_ready))]
            if rot_free[bank] is not None:
                w.append(rot_free[bank])
            E.emit("tensor", (lambda c=c, bank=bank: lambda eng: eng.matmul(
                P[bank][0:64, 0:512], A["kvsb"][0:64, c * 64:(c + 1) * 64],
                A["qbf"][0:64, c * 512:(c + 1) * 512], start=True, stop=True))(),
                waits=w)
            E.emit("tensor", (lambda c=c, bank=bank: lambda eng: eng.matmul(
                P[bank][64:128, 0:512], A["kvsb"][64:128, c * 64:(c + 1) * 64],
                A["qbf"][64:128, c * 512:(c + 1) * 512], start=True, stop=True))(),
                inc=sP)
            zbank = 6 + (c % 2)
            zslot = (c % 2) * 512
            zbw = [(sV, zinv_v), (qSel, sel_ld)]
            if tr_free[zbank] is not None:
                zbw.append(tr_free[zbank])
            zb_p = E.emit("tensor", (lambda c=c, zbank=zbank: lambda eng: eng.matmul(
                P[zbank][:, 0:512], A["selA"][:, c * 128:(c + 1) * 128], A["zsb"][:],
                start=True, stop=True))(), waits=zbw, inc=sP)
            ztile = "zbsb2" if c % 2 == 0 else "zbsbB"
            E.emit("vector", (lambda zbank=zbank, ztile=ztile: lambda eng: eng.tensor_copy(
                A[ztile][:, 0:512], P[zbank][:, 0:512]))(),
                waits=[(sP, zb_p)], inc=sV)
            zb_v = E.sem_val(sV)
            tr_free[zbank] = (sV, zb_v)
            E.emit("vector", (lambda c=c, bank=bank, ztile=ztile: lambda eng: eng.tensor_tensor(
                A["obf"][:, c * 512:(c + 1) * 512], P[bank][:, 0:512],
                A[ztile][:, 0:512], ALU.mult))(), inc=sV)
            o_done_v = E.sem_val(sV)
            rot_free[bank] = (sV, o_done_v)

        o0s, o0t = load_chunk(wb["wo_t"][:, 0:4, :], "wo_t")
        o1s, o1t = load_chunk(wb["wo_t"][:, 4:8, :], "wo_t")
        a_v = 0
        bank_i = 0
        for s in range(NS):
            for g in range(2):
                bank = bank_i % 4
                bank_i += 1
                mats = []
                for c in range(DC):
                    slot = o0s if c < 4 else o1s
                    cc = c % 4
                    mats.append((
                        (lambda c=c, s=s: A["obf"][:, c * 512 + s * 128: c * 512 + s * 128 + 128]),
                        (lambda slot=slot, cc=cc, g=g: A[f"wbuf{slot}"][:, cc * 1024 + g * 512: cc * 1024 + g * 512 + 512]),
                    ))
                g_done = mm_group(bank, mats,
                                  waits=[(sV, o_done_v), chunk_wait(o0t), chunk_wait(o1t)])
                off = s * 1024 + g * 512
                E.emit("vector", (lambda bank=bank, off=off: lambda eng: eng.tensor_tensor(
                    A["res1"][:, off:off + 512], A["res1"][:, off:off + 512],
                    P[bank][:, 0:512], ALU.add))(), waits=[(sP, g_done)], inc=sV)
                a_v = E.sem_val(sV)
                rot_free[bank] = (sV, a_v)
        chunk_consumed(o0t, o0s, sV, a_v)
        chunk_consumed(o1t, o1s, sV, a_v)
        for s in range(NS):
            E.emit("vector", (lambda s=s: lambda eng: eng.tensor_tensor(
                A["res1"][:, s * 1024:(s + 1) * 1024],
                A["res1"][:, s * 1024:(s + 1) * 1024], A["boB"][:], ALU.add))(), inc=sV)
        a_v = E.sem_val(sV)
        if stage == "a":
            return dbg_store(t, A["res1"], a_v)

        E.barrier("scalar", [(sV, a_v)])
        stats1 = ln_stats(A["res1"], A["res2"])
        n1_v = ln_apply(A["res1"], A["res2"], stats1)
        for s in range(NS):
            E.emit("vector", (lambda s=s: lambda eng: eng.tensor_tensor(
                A["res2"][:, s * 1024:(s + 1) * 1024],
                A["res2"][:, s * 1024:(s + 1) * 1024], A["g1B"][:], ALU.mult))(), inc=sV)
            E.emit("vector", (lambda s=s: lambda eng: eng.tensor_tensor(
                A["res2"][:, s * 1024:(s + 1) * 1024],
                A["res2"][:, s * 1024:(s + 1) * 1024], A["be1B"][:], ALU.add))(), inc=sV)
        x1_v = E.sem_val(sV)
        if stage == "x1":
            return dbg_store(t, A["res2"], x1_v)
        fmT1_v = tm_to_fm(A["res2"], "p2", [(sV, x1_v)], A["fmT"])

        flat1 = wb["w1_s"].rearrange("a p d -> p a d")
        g_v = 0
        w1tok = [None, None]
        for cq in range(FC):
            if cq % 4 == 0:
                i = cq // 4
                w1tok = list(load_chunk(flat1[:, i * 4:(i + 1) * 4, :], "w1_s"))
            bank = cq % 4
            slot, tok = w1tok
            ccol = (cq % 4) * 1024
            mats = [((lambda slot=slot, ccol=ccol, c=c: A[f"wbuf{slot}"][:, ccol + c * 128: ccol + (c + 1) * 128]),
                     (lambda c=c: A["fmT"][:, c * 512:(c + 1) * 512]))
                    for c in range(DC)]
            g_done = mm_group(bank, mats, waits=[(sV, fmT1_v), chunk_wait(tok)])
            E.emit("scalar", (lambda bank=bank, cq=cq: lambda eng: eng.activation(
                A["bigA"][:, cq * 512:(cq + 1) * 512], P[bank][:, 0:512], AF.Gelu,
                bias=A["b1_c"][:, cq:cq + 1], scale=1.0))(),
                waits=[(sP, g_done)], inc=sS)
            g_v = E.sem_val(sS)
            rot_free[bank] = (sS, g_v)
            if cq % 4 == 3:
                chunk_consumed(tok, slot, sS, g_v)
        E.emit("scalar", lambda eng: eng.drain(), inc=sS)
        g_v = E.sem_val(sS)

        y_v = 0
        for cq in range(DC):
            slot, tok = load_chunk(wb["w2_s"][cq], "w2_s")
            bank = cq % 4
            mats = [((lambda slot=slot, c=c: A[f"wbuf{slot}"][:, c * 128:(c + 1) * 128]),
                     (lambda c=c: A["bigA"][:, c * 512:(c + 1) * 512]))
                    for c in range(FC)]
            g_done = mm_group(bank, mats, waits=[(sS, g_v), chunk_wait(tok)])
            E.emit("vector", (lambda bank=bank, cq=cq: lambda eng: eng.tensor_scalar(
                A["scrF"][:, cq * 512:(cq + 1) * 512], P[bank][:, 0:512],
                A["b2_c"][:, cq:cq + 1], 0.0, ALU.add, ALU.add))(),
                waits=[(sP, g_done)], inc=sV)
            y_v = E.sem_val(sV)
            rot_free[bank] = (sV, y_v)
            chunk_consumed(tok, slot, sV, y_v)

        if stage == "yfm":
            return dbg_store(t, A["scrF"], y_v)
        r1_v = fm_to_tm(A["scrF"], "p2", [(sV, y_v)], A["res2"], add=True)
        if stage == "r1":
            return dbg_store(t, A["res2"], r1_v)

        E.barrier("scalar", [(sV, r1_v)])
        stats2 = ln_stats(A["res2"], A["res1"])
        n2_v = ln_apply(A["res2"], A["res2"], stats2)
        for s in range(NS):
            E.emit("vector", (lambda s=s: lambda eng: eng.tensor_tensor(
                A["res2"][:, s * 1024:(s + 1) * 1024],
                A["res2"][:, s * 1024:(s + 1) * 1024], A["g2B"][:], ALU.mult))(), inc=sV)
            E.emit("vector", (lambda s=s: lambda eng: eng.tensor_tensor(
                A["res2"][:, s * 1024:(s + 1) * 1024],
                A["res2"][:, s * 1024:(s + 1) * 1024], A["be2B"][:], ALU.add))(), inc=sV)
        blk_v = E.sem_val(sV)
        E.barrier("scalar", [(sV, blk_v)])
        stats3 = ln_stats(A["res2"], A["res1"])
        v_v = ln_apply(A["res2"], A["res2"], stats3)
        for s in range(NS):
            E.emit("vector", (lambda s=s: lambda eng: eng.tensor_tensor(
                A["res1"][:, s * 1024:(s + 1) * 1024],
                A["xbuf"][:, s * 1024:(s + 1) * 1024],
                A["res2"][:, s * 1024:(s + 1) * 1024], ALU.add))(), inc=sV)
        x2_v = E.sem_val(sV)
        if stage == "x2":
            return dbg_store(t, A["res1"], x2_v)

        E.barrier("scalar", [(sV, x2_v)])
        stats4 = ln_stats(A["res1"], A["res2"])
        ln2_v = ln_apply(A["res1"], A["res2"], stats4)
        if stage == "ln2t":
            return dbg_store(t, A["res2"], ln2_v)
        fmT2_v = tm_to_fm(A["res2"], "p2", [(sV, ln2_v)], A["fmT"])

        early_v = None
        ex_ld = None
        if t + 1 < n_tiles:
            E.emit("scalar", (lambda t=t: lambda eng: eng.dma_start(
                out=xbuf_3d[:], in_=x_tiled[t + 1]))(),
                waits=[(sV, x2_v)], inc=qX, amt=16)
            ex_ld = E.sem_val(qX)

        flatw1 = wb["fw1_s"].rearrange("a p d -> p a d")
        flatw2 = wb["fw2_s"].rearrange("a p d -> p a d")
        t1_v = 0
        gtoks = [None] * 4
        for cq in range(FC):
            if cq % 4 == 0:
                i = cq // 4
                gtoks[0], gtoks[1] = load_chunk(flatw1[:, i * 4:(i + 1) * 4, :], "fw1_s")
                gtoks[2], gtoks[3] = load_chunk(flatw2[:, i * 4:(i + 1) * 4, :], "fw2_s")
            bankA = (2 * cq) % 4
            bankB = (2 * cq + 1) % 4
            ccol = (cq % 4) * 1024
            mats1 = [((lambda slot=gtoks[0], ccol=ccol, c=c: A[f"wbuf{slot}"][:, ccol + c * 128: ccol + (c + 1) * 128]),
                      (lambda c=c: A["fmT"][:, c * 512:(c + 1) * 512]))
                     for c in range(DC)]
            mats2 = [((lambda slot=gtoks[2], ccol=ccol, c=c: A[f"wbuf{slot}"][:, ccol + c * 128: ccol + (c + 1) * 128]),
                      (lambda c=c: A["fmT"][:, c * 512:(c + 1) * 512]))
                     for c in range(DC)]
            gd1 = mm_group(bankA, mats1, waits=[(sV, fmT2_v), chunk_wait(gtoks[1])])
            gd2 = mm_group(bankB, mats2, waits=[(sV, fmT2_v), chunk_wait(gtoks[3])])
            E.emit("scalar", (lambda bankA=bankA, cq=cq: lambda eng: eng.activation(
                A["scr4k"][:, 0:512], P[bankA][:, 0:512], AF.Silu,
                bias=0.0, scale=A["s1_c"][:, cq:cq + 1]))(),
                waits=[(sP, gd1), (sV, t1_v)], inc=sS, flush=True)
            sil_v = E.sem_val(sS)
            rot_free[bankA] = (sS, sil_v)
            E.emit("vector", (lambda bankB=bankB, cq=cq: lambda eng: eng.scalar_tensor_tensor(
                A["bigA"][:, cq * 512:(cq + 1) * 512], P[bankB][:, 0:512],
                A["s2_c"][:, cq:cq + 1], A["scr4k"][:, 0:512], ALU.mult, ALU.mult))(),
                waits=[(sP, gd2), (sS, sil_v)], inc=sV)
            t1_v = E.sem_val(sV)
            rot_free[bankB] = (sV, t1_v)
            if cq % 4 == 3:
                chunk_consumed(gtoks[1], gtoks[0], sV, t1_v)
                chunk_consumed(gtoks[3], gtoks[2], sV, t1_v)

        if ex_ld is not None:
            E.barrier("vector", [(qX, ex_ld)])
            E.barrier("scalar", [(qX, ex_ld)])
            early_v = ln_stats(A["xbuf"], A["scrF"], base=16)
            state["early"] = (early_v, (qX, ex_ld))

        def big_ff(wname, src, dst, scol, prev, first_extra=()):
            last_s = 0
            for cq in range(FC):
                slot, tok = load_chunk(wb[wname][cq], wname)
                bank = cq % 6
                mats = [((lambda slot=slot, c=c: A[f"wbuf{slot}"][:, c * 128:(c + 1) * 128]),
                         (lambda c=c, src=src: src[:, c * 512:(c + 1) * 512]))
                        for c in range(FC)]
                g_done = mm_group(bank, mats, waits=[prev, chunk_wait(tok)])
                ws = [(sP, g_done)] + (list(first_extra) if cq == 0 else [])
                E.emit("scalar", (lambda bank=bank, cq=cq, dst=dst, scol=scol:
                                  lambda eng: eng.activation(
                    dst[:, cq * 512:(cq + 1) * 512], P[bank][:, 0:512], AF.Gelu,
                    bias=0.0, scale=A[scol][:, cq:cq + 1]))(),
                    waits=ws, inc=sS)
                last_s = E.sem_val(sS)
                rot_free[bank] = (sS, last_s)
                chunk_consumed(tok, slot, sS, last_s)
            E.emit("scalar", lambda eng: eng.drain(), inc=sS)
            return E.sem_val(sS)

        t2_v = big_ff("h1_s", A["bigA"], A["bigB"], "sh1_c", (sV, t1_v),
                      first_extra=([state["store_done"]] if state.get("store_done") else []))
        t3_v = big_ff("h2_s", A["bigB"], A["bigA"], "sh2_c", (sS, t2_v))

        f_v = 0
        for cq in range(DC):
            slot, tok = load_chunk(wb["w3_s"][cq], "w3_s")
            bank = cq % 4
            mats = [((lambda slot=slot, c=c: A[f"wbuf{slot}"][:, c * 128:(c + 1) * 128]),
                     (lambda c=c: A["bigA"][:, c * 512:(c + 1) * 512]))
                    for c in range(FC)]
            g_done = mm_group(bank, mats, waits=[(sS, t3_v), chunk_wait(tok)])
            wv = [(sP, g_done)]
            if cq == 0 and early_v is not None:
                wv.append((sV, early_v))
            E.emit("vector", (lambda bank=bank, cq=cq: lambda eng: eng.tensor_scalar(
                A["scrF"][:, cq * 512:(cq + 1) * 512], P[bank][:, 0:512],
                A["s3_c"][:, cq:cq + 1], 0.0, ALU.mult, ALU.add))(),
                waits=wv, inc=sV)
            f_v = E.sem_val(sV)
            rot_free[bank] = (sV, f_v)
            chunk_consumed(tok, slot, sV, f_v)
        state["sel_gate"] = (sV, f_v)

        if stage == "ffm":
            return dbg_store(t, A["scrF"], f_v)
        ftm_v = fm_to_tm(A["scrF"], "p2", [(sV, f_v)], A["res2"], add=False)
        E.barrier("scalar", [(sV, ftm_v)])
        stats5 = ln_stats(A["res2"], A["scrF"])
        nf_v = ln_apply(A["res2"], A["res2"], stats5)
        for s in range(NS):
            E.emit("vector", (lambda s=s: lambda eng: eng.tensor_tensor(
                A["outstg"][:, s * 1024:(s + 1) * 1024],
                A["res1"][:, s * 1024:(s + 1) * 1024],
                A["res2"][:, s * 1024:(s + 1) * 1024], ALU.add))(), inc=sV)
        out_v = E.sem_val(sV)

        E.emit("scalar", (lambda t=t: lambda eng: eng.dma_start(
            out=out_tiled[t],
            in_=A["outstg"].rearrange("p (s d) -> p s d", d=D)))(),
            waits=[(sV, out_v)], inc=qX, amt=16)
        state["store_done"] = (qX, E.sem_val(qX))
        state["xbuf_free"] = (sV, out_v)
        state["res_free"] = (sV, out_v)

    for t in range(n_tiles):
        pass2_tile(t)

    E.barrier("scalar", [(qX, E.sem_val(qX))])
    return finish(nc, E, stack)


def finish(nc, E, stack):
    with nc.Block() as blk:
        def make_body(eng_name):
            def body(eng):
                for fn, waits, inc in E.ev[eng_name]:
                    for sem, val in waits:
                        eng.wait_ge(sem, val)
                    if fn is None:
                        assert inc is None
                        continue
                    inst = fn(eng)
                    if inc is not None:
                        inst.then_inc(inc[0], inc[1])
            return body
        blk.sync(make_body("sync"))
        blk.gpsimd(make_body("gpsimd"))
        blk.scalar(make_body("scalar"))
        blk.vector(make_body("vector"))
        blk.tensor(make_body("tensor"))
    stack.close()
    return nc


def _prep_inputs(ins):
    f32 = np.float32

    def fm_swizzle(w_in_out, co_n, ci_n):
        a = np.asarray(w_in_out, f32).reshape(ci_n, 128, co_n, 128)
        return np.ascontiguousarray(a.transpose(2, 1, 0, 3).reshape(co_n, 128, ci_n * 128))

    def tm_swizzle(w_in_out):
        a = np.asarray(w_in_out, f32).reshape(8, 128, -1)
        return np.ascontiguousarray(a.transpose(1, 0, 2))

    def colmat(v, n):
        return np.ascontiguousarray(np.asarray(v, f32).reshape(n, 128).T)

    d = {}
    d["wk_t"] = tm_swizzle(ins["L0_attn_Wk"])
    d["wv_t"] = tm_swizzle(ins["L0_attn_Wv"])
    d["wo_t"] = tm_swizzle(ins["L0_attn_Wo"])
    d["wq_s"] = fm_swizzle(ins["L0_attn_Wq"], 8, 8)
    d["w1_s"] = fm_swizzle(ins["L0_attn_W1"], 32, 8)
    d["w2_s"] = fm_swizzle(ins["L0_attn_W2"], 8, 32)
    d["fw1_s"] = fm_swizzle(np.asarray(ins["L0_ffn_w1_w"], f32).T, 32, 8)
    d["fw2_s"] = fm_swizzle(np.asarray(ins["L0_ffn_w2_w"], f32).T, 32, 8)
    d["h1_s"] = fm_swizzle(np.asarray(ins["L0_ffn_h1_w"], f32).T, 32, 32)
    d["h2_s"] = fm_swizzle(np.asarray(ins["L0_ffn_h2_w"], f32).T, 32, 32)
    d["w3_s"] = fm_swizzle(np.asarray(ins["L0_ffn_w3_w"], f32).T, 8, 32)
    for k, src in [("bk", "L0_attn_bk"), ("bv", "L0_attn_bv"), ("bo", "L0_attn_bo"),
                   ("g1", "L0_attn_g1"), ("be1", "L0_attn_be1"),
                   ("g2", "L0_attn_g2"), ("be2", "L0_attn_be2")]:
        d[k] = np.ascontiguousarray(np.asarray(ins[src], f32))
    d["bq_c"] = colmat(ins["L0_attn_bq"], 8)
    d["b1_c"] = colmat(ins["L0_attn_b1"], 32)
    d["b2_c"] = colmat(ins["L0_attn_b2"], 8)
    d["s1_c"] = colmat(ins["L0_ffn_w1_s"], 32)
    d["s2_c"] = colmat(ins["L0_ffn_w2_s"], 32)
    d["sh1_c"] = colmat(ins["L0_ffn_h1_s"], 32)
    d["sh2_c"] = colmat(ins["L0_ffn_h2_s"], 32)
    d["s3_c"] = colmat(ins["L0_ffn_w3_s"], 8)
    d["id_f32"] = np.eye(128, dtype=f32)
    sel = np.zeros((16, 1024), f32)
    for c in range(8):
        sel[2 * c, c * 128:c * 128 + 64] = 1.0
        sel[2 * c + 1, c * 128 + 64:c * 128 + 128] = 1.0
    d["sel_f32"] = sel
    return d


def kernel(**inputs):
    shared = _prep_inputs(inputs)
    x = np.asarray(inputs["L0_attn_src"], np.float32)
    nc = build("full")
    in_maps = []
    for b in range(8):
        m = dict(shared)
        m["x"] = np.ascontiguousarray(x[b])
        in_maps.append(m)
    res = run_bass_kernel_spmd(nc, in_maps, core_ids=list(range(8)))
    out = np.stack([np.asarray(r["out"]) for r in res.results], axis=0)
    return out.astype(np.float32)


if __name__ == "__main__":
    print("building program...")
    nc = build("full")
    print("n_instructions:", nc.n_instructions())
